# version 02
# baseline (speedup 1.0000x reference)
import numpy as np
import ml_dtypes
import concourse.bass as bass
import concourse.mybir as mybir
from concourse.bass_utils import run_bass_kernel_spmd
from concourse.replica_groups import maybe_share_collective_output_space
from contextlib import ExitStack

F32 = mybir.dt.float32
BF16 = mybir.dt.bfloat16
U32 = mybir.dt.uint32
AF = mybir.ActivationFunctionType
ALU = mybir.AluOpType
AX = mybir.AxisListType
BF = ml_dtypes.bfloat16

N_CORES = 8
T = 2048
NT = 16
D = 1024
CC = 8
F0 = 4096
NF0 = 32
H1 = 2048
NH1 = 16
E = 8
CAP = 2560.0

GROUPS_SEQ = [[0, 1, 2, 3], [4, 5, 6, 7]]
GROUPS_ALL = [[0, 1, 2, 3, 4, 5, 6, 7]]

ENGINES = ("sync", "tensor", "vector", "scalar", "gpsimd")

DYN_N = 38912
ATTN0_NT2 = NT
ATTN0_SKIP = ()
MOE0_NE = E
MOE0_NTH = 2
MOE0_NGRP = 2
MOE0_HEAVY = False
MOE0_BAR = 0
MOE0_ESTART = 0
ATTN0_UPTO = 99


def _t(a):
    return np.ascontiguousarray(a)



def host_prep(inp):
    cm = {}
    x = np.asarray(inp["L0_attn_x"], dtype=np.float32)
    xf = x.reshape(16384, D)

    wqkv = np.asarray(inp["L0_attn_W_qkv"], dtype=np.float32)
    cm["wqkvT"] = _t(wqkv.T.reshape(CC, 128, 3072))
    wproj = np.asarray(inp["L0_attn_W_proj"], dtype=np.float32)
    cm["wprojT"] = _t(wproj.T.reshape(CC, 128, D))
    cm["bproj"] = _t(np.asarray(inp["L0_attn_b_proj"], np.float32).reshape(1, D))

    wr = np.asarray(inp["L0_ffn_Wr"], dtype=np.float32)
    cm["wr0"] = _t(wr.reshape(CC, 128, E))
    cm["br0"] = _t(np.asarray(inp["L0_ffn_br"], np.float32).reshape(1, E))
    for nm, key in (("wi0", "L0_ffn_Wi"), ("wg0", "L0_ffn_Wg")):
        w = np.asarray(inp[key], dtype=np.float32)
        cm[nm] = _t(w.reshape(E, CC, 128, NF0, 128).transpose(0, 3, 2, 1, 4))
    cm["wo0"] = _t(np.asarray(inp["L0_ffn_Wo"], np.float32)
                   .reshape(E, NF0, 128, D))
    for nm, key in (("bi0", "L0_ffn_bi"), ("bg0", "L0_ffn_bg")):
        b = np.asarray(inp[key], dtype=np.float32)
        cm[nm] = _t(b.reshape(E, NF0, 128).transpose(2, 0, 1).reshape(128, E * NF0))
    cm["bo0"] = _t(np.asarray(inp["L0_ffn_bo"], np.float32))
    g = np.asarray(inp["L0_ffn_gamma"], dtype=np.float32)
    b = np.asarray(inp["L0_ffn_beta"], dtype=np.float32)
    cm["gam0"] = _t(np.broadcast_to(g[:, None, :], (E, 128, D)))
    cm["bet0"] = _t(np.broadcast_to(b[:, None, :], (E, 128, D)))

    for nm, key in (("wq1", "L1_attn_Wq"), ("wk1", "L1_attn_Wk")):
        w = np.asarray(inp[key], dtype=np.float32)
        wt = w.T.reshape(CC, 128, CC, 128)
        cm[nm] = _t(wt.transpose(2, 1, 0, 3))
    for nm, key in (("wv1", "L1_attn_Wv"), ("wo1", "L1_attn_Wo")):
        w = np.asarray(inp[key], dtype=np.float32)
        cm[nm] = _t(w.T.reshape(CC, 128, D))
    cm["bv1row"] = _t(np.asarray(inp["L1_attn_bv"], np.float32).reshape(1, D))
    for nm, key in (("bq1", "L1_attn_bq"), ("bk1", "L1_attn_bk"),
                    ("bv1", "L1_attn_bv")):
        cm[nm] = _t(np.asarray(inp[key], np.float32).reshape(CC, 128).T)
    cm["bo1"] = _t(np.asarray(inp["L1_attn_bo"], np.float32).reshape(1, D))

    gw = np.asarray(inp["L1_ffn_gate_w"], dtype=np.float32)
    cm["gw1"] = _t(gw.T.reshape(CC, 128, E))
    for nm, key in (("wg1", "L1_ffn_gate_proj_w"), ("wu1", "L1_ffn_up_proj_w")):
        w = np.asarray(inp[key], dtype=np.float32)
        wt = w.transpose(0, 2, 1).reshape(E, CC, 128, NH1, 128)
        cm[nm] = _t(wt.transpose(0, 3, 2, 1, 4))
    wd = np.asarray(inp["L1_ffn_down_proj_w"], dtype=np.float32)
    cm["wd1"] = _t(wd.transpose(0, 2, 1).reshape(E, NH1, 128, D))

    cm["idn"] = np.eye(128, dtype=BF)
    tri = np.tril(np.ones((128, 128), dtype=np.float32))
    cm["mask_qk"] = tri.astype(BF)
    cm["mask_kq"] = _t(tri.T).astype(BF)
    cm["lincl"] = _t(np.triu(np.ones((128, 128), dtype=np.float32)))
    cm["ones_col"] = np.ones((128, 1), dtype=BF)
    cm["ones1"] = np.ones((1, 128), dtype=BF)
    cm["ones1f"] = np.ones((1, 128), dtype=np.float32)
    cm["onesf"] = np.ones((128, 1), dtype=np.float32)
    cm["iota8"] = _t(np.broadcast_to(np.arange(8, dtype=np.float32)[None, :],
                                     (128, 8)))
    cm["epsc"] = _t(np.broadcast_to(
        np.array([1e-5, 1e-12], dtype=np.float32)[None, :], (128, 2)))
    cm["idnf"] = np.eye(128, dtype=np.float32)

    per_core = []
    for c in range(N_CORES):
        d = dict(cm)
        d["x_in"] = _t(xf[c * T:(c + 1) * T])
        grp = GROUPS_SEQ[0] if c < 4 else GROUPS_SEQ[1]
        pm = np.array([1.0 if (r in grp and r < c) else 0.0 for r in range(8)],
                      dtype=np.float32)
        d["pref_mask"] = _t(np.broadcast_to(pm[None, :], (128, 8)))
        om = (np.arange(8) < c).astype(np.float32)
        d["off_mask"] = _t(np.broadcast_to(om[:, None], (8, 128)))
        per_core.append(d)
    return per_core



class Ctx:
    ESEM = {"tensor": "pe", "vector": "dv", "scalar": "ac", "gpsimd": "po"}

    def __init__(self, nc, es):
        self.nc = nc
        self.es = es
        self.q = {e: [] for e in ENGINES}
        self.sems = {}
        self.cnt = {}
        for name in ("pe", "dv", "ac", "po", "sy", "gp", "st",
                     "c0", "c1", "c2"):
            self.sems[name] = es.enter_context(nc.semaphore(name))
            self.cnt[name] = 0
        self.sb = {}
        self.dr = {}
        self.ps = {}

    def do(self, eng, fname, *a, inc=None, incn=1, waits=(), **k):
        sems = self.sems
        waits = tuple(w for w in waits if w is not None)

        def closure(e, fname=fname, a=a, k=k, waits=waits, inc=inc, incn=incn):
            for s, v in waits:
                e.wait_ge(sems[s], v)
            r = getattr(e, fname)(*a, **k)
            if inc is not None:
                r.then_inc(sems[inc], incn)
        self.q[eng].append(closure)
        if inc is not None:
            self.cnt[inc] += incn
            return (inc, self.cnt[inc])
        return None

    def wait(self, eng, *marks):
        marks = tuple(m for m in marks if m is not None)
        if not marks:
            return
        sems = self.sems

        def closure(e, marks=marks):
            for s, v in marks:
                e.wait_ge(sems[s], v)
        self.q[eng].append(closure)

    def tok(self, sem):
        return (sem, self.cnt[sem])

    def barrier(self, engines=("tensor", "vector", "scalar", "gpsimd")):
        marks = []
        for eng in engines:
            sem = self.ESEM[eng]
            self.do(eng, "nop", inc=sem)
            marks.append(self.tok(sem))
        for eng in engines:
            self.wait(eng, *marks)

    def finalize(self):
        with self.nc.Block() as block:
            for eng in ENGINES:
                ops = self.q[eng]
                if not ops:
                    continue

                def body(e, ops=ops):
                    for f in ops:
                        f(e)
                getattr(block, eng)(body)


def later(a, b):
    if a is None:
        return b
    if b is None:
        return a
    assert a[0] == b[0]
    return a if a[1] >= b[1] else b



DUMP_SHAPES = {
    "xn0": ([128, CC * T], BF16), "xn1": ([128, CC * T], BF16),
    "x_a0": ([128, NT * D], F32), "x_m0": ([128, NT * D], F32),
    "x_a1": ([128, NT * D], F32), "x_m1": ([128, NT * D], F32),
    "vk_dbg": ([128, 264], F32),
    "c0_dbg": ([128, NT * E], F32), "c1_dbg": ([128, NT * E], F32),
    "st_dbg": ([128, 520], F32),
    "h_dbg": ([128, 32768], BF16),
    "r_dbg": ([128, 1024], F32),
}


def build_program(phases, dump=()):
    nc = bass.Bass("TRN2", target_bir_lowering=False)
    es = ExitStack()
    ctx = Ctx(nc, es)
    ctx.dump = set(dump)
    need = set(phases)

    in_names = []

    def din(name, shape, dtype=F32):
        in_names.append(name)
        ctx.dr[name] = nc.dram_tensor(name, shape, dtype, kind="ExternalInput").ap()

    def dout(name, shape, dtype=F32):
        ctx.dr[name] = nc.dram_tensor(name, shape, dtype, kind="ExternalOutput").ap()

    din("x_in", [T, D])
    dout("x_out", [T, D])
    for nm in ctx.dump:
        dout(nm, *DUMP_SHAPES[nm])
    for cn, sh, dt in (("idn", [128, 128], BF16), ("mask_qk", [128, 128], BF16),
                       ("mask_kq", [128, 128], BF16), ("lincl", [128, 128], F32),
                       ("ones_col", [128, 1], BF16), ("ones1", [1, 128], BF16),
                       ("ones1f", [1, 128], F32),
                       ("onesf", [128, 1], F32), ("iota8", [128, 8], F32),
                       ("epsc", [128, 2], F32), ("idnf", [128, 128], F32),
                       ("pref_mask", [128, 8], F32), ("off_mask", [8, 128], F32)):
        din(cn, sh, dt)
    if "attn0" in need:
        din("wqkvT", [CC, 128, 3072])
        din("wprojT", [CC, 128, D])
        din("bproj", [1, D])
        ctx.dr["vk_in"] = nc.dram_tensor("vk_in", [128, 264], F32).ap()
        sp = maybe_share_collective_output_space("AllReduce", GROUPS_SEQ)
        ctx.dr["vk_out"] = nc.dram_tensor("vk_out", [128, 264], F32,
                                          addr_space=sp).ap()
    if "rt0" in need or "moe0" in need:
        din("wr0", [CC, 128, E])
        din("br0", [1, E])
    if "moe0" in need:
        din("wi0", [E, NF0, 128, CC, 128])
        din("wg0", [E, NF0, 128, CC, 128])
        din("wo0", [E, NF0, 128, D])
        din("bi0", [128, E * NF0])
        din("bg0", [128, E * NF0])
        din("bo0", [E, D])
        din("gam0", [E, 128, D])
        din("bet0", [E, 128, D])
    if "attn1" in need:
        for w in ("wq1", "wk1"):
            din(w, [CC, 128, CC, 128])
        for w in ("wv1", "wo1"):
            din(w, [CC, 128, D])
        for bnm in ("bq1", "bk1"):
            din(bnm, [128, CC])
        din("bv1row", [1, D])
        din("bo1", [1, D])
        ctx.dr["s_in"] = nc.dram_tensor("s_in", [128, 520], F32).ap()
        sp = maybe_share_collective_output_space("AllGather", GROUPS_ALL)
        ctx.dr["s_out"] = nc.dram_tensor("s_out", [8 * 128, 520], F32,
                                         addr_space=sp).ap()
    if "rt1" in need or "moe1" in need:
        din("gw1", [CC, 128, E])
        ctx.dr["r_in"] = nc.dram_tensor("r_in", [1, 16], F32).ap()
        sp = maybe_share_collective_output_space("AllGather", GROUPS_ALL)
        ctx.dr["r_out"] = nc.dram_tensor("r_out", [8, 16], F32,
                                         addr_space=sp).ap()
    if "moe1" in need:
        din("wg1", [E, NH1, 128, CC, 128])
        din("wu1", [E, NH1, 128, CC, 128])
        din("wd1", [E, NH1, 128, D])

    def sb(name, shape, dtype):
        h = es.enter_context(nc.sbuf_tensor(name, shape, dtype))
        ctx.sb[name] = h
        return h

    sb("x", [128, NT * D], F32)
    sb("xn", [128, CC * T], BF16)
    sb("lnst", [128, 64], F32)
    sb("dyn", [128, DYN_N], BF16)
    sb("rt", [128, D], F32)
    sb("ftmp", [128, 1024], F32)
    sb("tmp", [128, 2048], BF16)
    sb("gamb", [128, D], F32)
    sb("betb", [128, D], F32)
    sb("borow", [4, D], BF16)
    sb("cst", [128, 128], BF16)
    sb("mqk", [128, 128], BF16)
    sb("mkq", [128, 128], BF16)
    sb("linc", [128, 128], F32)
    sb("onec", [128, 1], BF16)
    sb("one1", [1, 128], BF16)
    sb("one1f", [1, 128], F32)
    sb("onef", [128, 1], F32)
    sb("io8", [128, 8], F32)
    sb("pmask", [128, 8], F32)
    sb("omask", [8, 128], F32)
    sb("bsm", [128, 32], F32)
    sb("bib", [128, E * NF0], F32)
    sb("bgh", [128, E * NF0], F32)
    sb("call", [128, NT * E], F32)
    sb("c1al", [128, NT * E], F32)
    sb("m12", [128, NT * 16], F32)
    sb("wn12", [128, NT * 2], F32)
    sb("rowt", [8, 16], F32)
    sb("carry", [1, 16], F32)
    sb("cnt16", [128, 16], F32)
    sb("sml", [128, 32], F32)
    sb("epss", [128, 2], F32)
    sb("wrsb", [128, CC * E], BF16)
    sb("brrow", [1, 16], BF16)
    sb("bvrow", [1, D], BF16)
    sb("idnfs", [128, 128], F32)
    sb("wrf", [128, 64], F32)
    sb("brf", [1, 16], F32)

    dynh = ctx.sb["dyn"]
    ctx.sb["vk"] = dynh[:, 37120:37648].bitcast(F32)
    ctx.sb["vkbf"] = dynh[:, 37648:37912]
    ctx.sb["Ssb"] = dynh[:, 37120:38160].bitcast(F32)
    ctx.sb["Sbf"] = dynh[:, 38160:38680]
    ctx.sb["gath"] = dynh[:, 0:1040].bitcast(F32)

    pb = []
    for i in range(8):
        if i >= 6:
            h = es.enter_context(nc.psum_tensor("pt%d" % i, [128, 1024], BF16))
        else:
            h = es.enter_context(nc.psum_tensor("pb%d" % i, [128, 512], F32))
        pb.append(h)
    ctx.pb = pb

    def bank(i):
        if i >= 6:
            return pb[i][:].bitcast(F32)
        return pb[i][:]

    def bankb(i):
        return pb[i][:]

    ctx.bank = bank
    ctx.bankb = bankb

    S = ctx.sb

    x3 = ctx.dr["x_in"].rearrange("(a p) d -> p a d", p=128)
    xs3 = S["x"][:].rearrange("p (a d) -> p a d", d=D)
    for half in range(2):
        ctx.do("sync", "dma_start",
               out=xs3[:, half * 8:(half + 1) * 8, :],
               in_=x3[:, half * 8:(half + 1) * 8, :],
               inc="sy", incn=16)
    for dst, src in (("cst", "idn"), ("mqk", "mask_qk"), ("mkq", "mask_kq"),
                     ("linc", "lincl"), ("onec", "ones_col"), ("one1", "ones1"),
                     ("one1f", "ones1f"),
                     ("onef", "onesf"), ("io8", "iota8"), ("pmask", "pref_mask"),
                     ("epss", "epsc"), ("idnfs", "idnf"),
                     ("omask", "off_mask")):
        ctx.do("sync", "dma_start", out=S[dst][:], in_=ctx.dr[src],
               inc="sy", incn=16)
    boot = ctx.tok("sy")

    st = dict(boot=boot, x_ready=boot, x_writes=[], xn_consumers=[])

    for ph in phases:
        PHASE_EMITTERS[ph](ctx, st)

    ctx.wait("sync", *st["x_writes"])
    xo3 = ctx.dr["x_out"].rearrange("(a p) d -> p a d", p=128)
    for half in range(2):
        ctx.do("sync", "dma_start",
               out=xo3[:, half * 8:(half + 1) * 8, :],
               in_=xs3[:, half * 8:(half + 1) * 8, :],
               inc="st", incn=16)
    ctx.wait("sync", ctx.tok("st"))

    ctx.finalize()
    return nc, es, in_names


def xt(S, tc):
    return S["x"][:, tc * D:(tc + 1) * D]


def xnf(S, cc, lo, n):
    return S["xn"][:, cc * T + lo:cc * T + lo + n]


def dmaw(ctx, dst_ap, src_ap):
    return ctx.do("gpsimd", "dma_start", out=dst_ap, in_=src_ap,
                  inc="gp", incn=16)


def dump_buf(ctx, st, name, ap, waitmark):
    if name in ctx.dump:
        ctx.wait("sync", waitmark)
        ctx.do("sync", "dma_start", out=ctx.dr[name], in_=ap, inc="st", incn=16)



def emit_ln(ctx, st, layer):
    S = ctx.sb
    ctx.wait("vector", *st["x_writes"], st["x_ready"])
    for tc in range(NT):
        ctx.do("vector", "bn_stats", S["lnst"][:, 48:54], xt(S, tc)[:, 0:512])
        ctx.do("vector", "bn_stats", S["lnst"][:, 54:60], xt(S, tc)[:, 512:1024])
        ctx.do("vector", "drain")
        ctx.do("vector", "bn_aggr", S["lnst"][:, 60:62], S["lnst"][:, 48:60])
        ctx.do("vector", "drain")
        ctx.do("vector", "tensor_copy", S["lnst"][:, tc:tc + 1],
               S["lnst"][:, 60:61])
        ctx.do("vector", "tensor_copy", S["lnst"][:, 16 + tc:17 + tc],
               S["lnst"][:, 61:62], inc="dv")
    mvar = ctx.tok("dv")
    ctx.wait("scalar", mvar)
    ctx.do("scalar", "activation", S["lnst"][:, 32:48], S["lnst"][:, 16:32],
           AF.Sqrt, bias=S["epss"][:, 0:1], scale=1.0, inc="ac")
    ctx.wait("vector", ctx.tok("ac"))
    ctx.do("vector", "reciprocal", S["lnst"][:, 16:32], S["lnst"][:, 32:48])
    ctx.do("vector", "drain")
    ctx.do("vector", "tensor_mul", S["lnst"][:, 32:48], S["lnst"][:, 0:16],
           S["lnst"][:, 16:32])
    ctx.do("vector", "drain")
    ctx.do("vector", "tensor_scalar", S["lnst"][:, 32:48], S["lnst"][:, 32:48],
           -1.0, None, ALU.mult, inc="dv")
    st["stats_ready"] = ctx.tok("dv")

    ctx.wait("vector", *st["xn_consumers"])
    pe_done = [None, None]
    dv_done = [None, None]
    for tc in range(NT):
        par = tc % 2
        buf = S["tmp"][:, par * 1024:(par + 1) * 1024]
        ctx.wait("vector", pe_done[par])
        ctx.do("vector", "tensor_scalar", buf, xt(S, tc),
               S["lnst"][:, tc:tc + 1], S["lnst"][:, 16 + tc:17 + tc],
               ALU.subtract, ALU.mult, inc="dv")
        m = ctx.tok("dv")
        pt = ctx.bankb(6 + par)
        ctx.wait("tensor", m, dv_done[par])
        for j in range(CC):
            ctx.do("tensor", "transpose", pt[:, j * 128:(j + 1) * 128],
                   buf[:, j * 128:(j + 1) * 128], S["cst"][:],
                   inc="pe" if j == CC - 1 else None)
        pe_done[par] = ctx.tok("pe")
        ctx.wait("vector", pe_done[par])
        for j in range(CC):
            ctx.do("vector", "tensor_copy",
                   xnf(S, j, tc * 128, 128), pt[:, j * 128:(j + 1) * 128],
                   inc="dv" if j == CC - 1 else None)
        dv_done[par] = ctx.tok("dv")
    st["xn_ready"] = ctx.tok("dv")
    st["xn_consumers"] = []
    dump_buf(ctx, st, "xn%d" % layer, S["xn"][:], st["xn_ready"])



def emit_attn0(ctx, st):
    S = ctx.sb
    nc = ctx.nc
    bank, bankb = ctx.bank, ctx.bankb
    QFM = 0
    WKV = 16384
    VST = 32768
    KST = 35072
    AST = 37120
    dyn = S["dyn"]

    def dslice(off, n):
        return dyn[:, off:off + n]

    wkv_m = dmaw(ctx, dslice(WKV, 16384).rearrange("p (c m) -> p c m", c=CC),
                 ctx.dr["wqkvT"][:, :, 1024:3072]
                 .rearrange("c p m -> p c m"))
    xn_rdy = st["xn_ready"]

    ctx.do("vector", "memset", S["vk"][:], 0.0)
    for par in range(2):
        v_ap = dslice(VST + par * 1088, 1056)
        ctx.do("vector", "memset",
               v_ap.rearrange("p (h t) -> p h t", t=33)[:, :, 32:33], 1.0,
               inc="dv" if par == 1 else None)
    vst_init = ctx.tok("dv")

    ctx.wait("tensor", wkv_m, xn_rdy, vst_init)
    kv_banks = [0, 1, 2, 3]
    vk_banks = [4, 5]
    ev_done = {}
    vk_done = {}
    for tc in range(NT):
        par = tc % 2
        kst = dslice(KST + par * 1024, 1024)
        vst = dslice(VST + par * 1088, 1056)
        seg_marks = []
        for seg in range(4):
            b = kv_banks[seg]
            ctx.wait("tensor", ev_done.get(b))
            for cc in range(CC):
                ctx.do("tensor", "matmul", bank(b),
                       xnf(S, cc, tc * 128, 128),
                       dslice(WKV + cc * 2048 + seg * 512, 512),
                       start=(cc == 0), stop=(cc == CC - 1),
                       inc="pe" if cc == CC - 1 else None)
            seg_marks.append(ctx.tok("pe"))
        for seg in range(2):
            b = kv_banks[seg]
            ctx.wait("scalar", seg_marks[seg], ev_done.get(("kwr", par)))
            ctx.do("scalar", "activation", kst[:, seg * 512:(seg + 1) * 512],
                   bank(b), AF.Relu, inc="ac")
            ev_done[b] = ctx.tok("ac")
        for seg in range(2, 4):
            b = kv_banks[seg]
            ctx.wait("vector", seg_marks[seg], ev_done.get(("vwr", par)))
            src = bank(b).rearrange("p (h t) -> p h t", t=32)
            dst = vst.rearrange("p (h t) -> p h t", t=33)[
                :, seg * 16 - 32:seg * 16 - 16, 0:32]
            ctx.do("vector", "tensor_copy", dst, src,
                   inc="dv" if seg == 3 else None)
        kv_rdy_k = ev_done[kv_banks[1]]
        kv_rdy_v = ctx.tok("dv")
        ev_done[kv_banks[2]] = kv_rdy_v
        ev_done[kv_banks[3]] = kv_rdy_v
        del kv_rdy_k
        for hg in range(4):
            b = vk_banks[hg % 2]
            pbase = hg * 32
            ctx.wait("tensor", ev_done[kv_banks[1]], kv_rdy_v, vk_done.get(b))
            for i in range(8):
                h = hg * 8 + i
                ctx.do("tensor", "matmul",
                       bank(b)[pbase:pbase + 32, i * 33:i * 33 + 33],
                       kst[:, h * 32:h * 32 + 32],
                       vst[:, h * 33:h * 33 + 33],
                       start=True, stop=True, tile_position=(0, pbase),
                       inc="pe" if i == 7 else None)
            mvk = ctx.tok("pe")
            ctx.wait("vector", mvk)
            ctx.do("vector", "tensor_add",
                   S["vk"][pbase:pbase + 32, 0:264],
                   S["vk"][pbase:pbase + 32, 0:264],
                   bank(b)[pbase:pbase + 32, 0:264], inc="dv")
            vk_done[b] = ctx.tok("dv")
        ev_done[("kwr", par)] = ctx.tok("pe")
        ev_done[("vwr", par)] = ctx.tok("pe")
    vk_local = ctx.tok("dv")
    dump_buf(ctx, st, "vk_dbg", S["vk"][:], vk_local)
    if ATTN0_UPTO < 2:
        st["xn_consumers"].append(ctx.tok("pe"))
        return

    pass1_pe = ctx.tok("pe")
    ctx.wait("gpsimd", pass1_pe)
    wq_m = dmaw(ctx, dslice(WKV, 8192).rearrange("p (c m) -> p c m", c=CC),
                ctx.dr["wqkvT"][:, :, 0:1024].rearrange("c p m -> p c m"))

    ctx.wait("sync", vk_local)
    ctx.do("sync", "dma_start", out=ctx.dr["vk_in"], in_=S["vk"][:],
           inc="sy", incn=16)
    ctx.wait("gpsimd", ctx.tok("sy"))
    ctx.do("gpsimd", "collective_compute", "AllReduce", ALU.add,
           replica_groups=GROUPS_SEQ,
           ins=[ctx.dr["vk_in"]], outs=[ctx.dr["vk_out"]], inc="c0")
    ctx.wait("gpsimd", ctx.tok("c0"))
    vo = ctx.dr["vk_out"].rearrange("(g e) (s d) -> g e s d", g=4, d=33)
    for b4 in range(4):
        for g in range(4):
            ap = vo[g, :, b4:b4 + 5:4, :]
            ctx.do("gpsimd", "dma_start",
                   out=S["vkbf"][b4 * 32:(b4 + 1) * 32,
                                 g * 66:(g + 1) * 66]
                   .rearrange("e (s d) -> e s d", d=33),
                   in_=ap, inc="gp", incn=16)
    vkbf_rdy = ctx.tok("gp")
    if ATTN0_UPTO < 3:
        ctx.wait("sync", vkbf_rdy)
        st["xn_consumers"].append(ctx.tok("pe"))
        return

    q_banks = [0, 1]
    trp = [6, 7]
    ev_q = {}
    pe_tr = {}
    dv_tr = {}
    ctx.wait("tensor", wq_m, pass1_pe)
    for tc in range(NT):
        par = tc % 2
        qst = dslice(KST + par * 1024, 1024)
        segm = []
        for seg in range(2):
            b = q_banks[seg]
            ctx.wait("tensor", ev_q.get(b))
            for cc in range(CC):
                ctx.do("tensor", "matmul", bank(b),
                       xnf(S, cc, tc * 128, 128),
                       dslice(WKV + cc * 1024 + seg * 512, 512),
                       start=(cc == 0), stop=(cc == CC - 1),
                       inc="pe" if cc == CC - 1 else None)
            segm.append(ctx.tok("pe"))
        for seg in range(2):
            b = q_banks[seg]
            ctx.wait("scalar", segm[seg], pe_tr.get(par))
            ctx.do("scalar", "activation", qst[:, seg * 512:(seg + 1) * 512],
                   bank(b), AF.Relu, inc="ac")
            ev_q[b] = ctx.tok("ac")
        mq = ctx.tok("ac")
        pt = bankb(trp[par])
        ctx.wait("tensor", mq, dv_tr.get(par))
        for j in range(CC):
            ctx.do("tensor", "transpose", pt[:, j * 128:(j + 1) * 128],
                   qst[:, j * 128:(j + 1) * 128], S["cst"][:],
                   inc="pe" if j == CC - 1 else None)
        pe_tr[par] = ctx.tok("pe")
        ctx.wait("vector", pe_tr[par])
        for j in range(CC):
            ctx.do("vector", "tensor_copy",
                   dslice(QFM + j * 2048 + tc * 128, 128),
                   pt[:, j * 128:(j + 1) * 128],
                   inc="dv" if j == CC - 1 else None)
        dv_tr[par] = ctx.tok("dv")
    q_rdy = ctx.tok("dv")
    if ATTN0_UPTO < 4:
        st["xn_consumers"].append(ctx.tok("pe"))
        return

    ctx.wait("gpsimd", pass1_pe)
    wp_m = dmaw(ctx, dslice(WKV + 8192, 8192).rearrange("p (c m) -> p c m", c=CC),
                ctx.dr["wprojT"].rearrange("c p m -> p c m"))
    bp_m = dmaw(ctx, S["borow"][0:1, :], ctx.dr["bproj"])
    ob_done = {}
    pr_done = None
    x_marks = []
    ctx.wait("tensor", q_rdy, vkbf_rdy)
    for tc in range(ATTN0_NT2):
        par = tc % 2
        atm = dslice(VST + par * 1088, 1056)[:, 0:1024]
        afm = dslice(KST + par * 1024, 1024)
        ctx.wait("tensor", *[ob_done.get(2 + j) for j in range(4)])
        for h in range(32):
            base = (h % 4) * 32
            b = 2 + (h % 4)
            s_ = h // 4
            vkc = (h // 8) * 2 + (h % 8) // 4
            ctx.do("tensor", "matmul",
                   bank(b)[:, s_ * 33:s_ * 33 + 33],
                   dyn[base:base + 32,
                       QFM + (h // 4) * 2048 + tc * 128:
                       QFM + (h // 4) * 2048 + tc * 128 + 128],
                   S["vkbf"][base:base + 32, vkc * 33:vkc * 33 + 33],
                   start=True, stop=True, tile_position=(base, 0),
                   inc="pe" if h == 31 else None)
        mg = ctx.tok("pe")
        ctx.wait("vector", mg)
        for j in range(4):
            b = 2 + j
            pv = bank(b)[:, 0:264].rearrange("p (h t) -> p h t", t=33)
            ctx.do("vector", "tensor_scalar", S["sml"][:, 0:8],
                   pv[:, :, 32], 1e-15, None, ALU.add)
            ctx.do("vector", "drain")
            ctx.do("vector", "reciprocal", S["sml"][:, 8:16], S["sml"][:, 0:8])
            ctx.do("vector", "drain")
            for s_ in range(8):
                h = s_ * 4 + j
                ctx.do("vector", "tensor_scalar",
                       atm[:, h * 32:h * 32 + 32],
                       bank(b)[:, s_ * 33:s_ * 33 + 32],
                       S["sml"][:, 8 + s_:9 + s_], None, ALU.mult,
                       inc="dv" if s_ == 7 else None)
            ob_done[b] = ctx.tok("dv")
        m_at = ctx.tok("dv")
        if "tr" in ATTN0_SKIP:
            continue
        pt = bankb(trp[par])
        ctx.wait("tensor", m_at, ob_done.get("dvaf%d" % par))
        for j in range(CC):
            ctx.do("tensor", "transpose", pt[:, j * 128:(j + 1) * 128],
                   atm[:, j * 128:(j + 1) * 128], S["cst"][:],
                   inc="pe" if j == CC - 1 else None)
        mtr = ctx.tok("pe")
        ctx.wait("vector", mtr)
        for j in range(CC):
            ctx.do("vector", "tensor_copy", afm[:, j * 128:(j + 1) * 128],
                   pt[:, j * 128:(j + 1) * 128],
                   inc="dv" if j == CC - 1 else None)
        ob_done["dvaf%d" % par] = ctx.tok("dv")
        maf = ctx.tok("dv")
        if "proj" in ATTN0_SKIP:
            continue
        b0, b1 = 0, 1
        ctx.wait("tensor", maf, wp_m, bp_m, pr_done)
        for half in range(2):
            bb = (b0, b1)[half]
            for j in range(CC):
                ctx.do("tensor", "matmul", bank(bb),
                       afm[:, j * 128:(j + 1) * 128],
                       dslice(WKV + 8192 + j * 1024 + half * 512, 512),
                       start=(j == 0), stop=False)
            ctx.do("tensor", "matmul", bank(bb), S["one1"][:],
                   S["borow"][0:1, half * 512:(half + 1) * 512],
                   start=False, stop=True,
                   inc="pe")
        mpr = ctx.tok("pe")
        ctx.wait("vector", mpr)
        ctx.do("vector", "tensor_add", xt(S, tc)[:, 0:512], xt(S, tc)[:, 0:512],
               bank(b0))
        ctx.do("vector", "tensor_add", xt(S, tc)[:, 512:1024],
               xt(S, tc)[:, 512:1024], bank(b1), inc="dv")
        pr_done = ctx.tok("dv")
        x_marks.append(pr_done)
    if x_marks:
        st["x_writes"] = [x_marks[-1]]
        dump_buf(ctx, st, "x_a0", S["x"][:], x_marks[-1])
    st["xn_consumers"].append(ctx.tok("pe"))
    st["bo_pe_free"] = ctx.tok("pe")
    st["dyn_pe_free"] = ctx.tok("pe")
    st["dyn_dv_free"] = ctx.tok("dv")





def emit_router0(ctx, st):
    S = ctx.sb
    bank = ctx.bank
    wr_m = ctx.do("sync", "dma_start",
                  out=S["wrf"][:].rearrange("p (c e) -> p c e", e=E),
                  in_=ctx.dr["wr0"].rearrange("c p e -> p c e"),
                  inc="sy", incn=16)
    br_m = ctx.do("sync", "dma_start", out=S["brf"][0:1, 0:E],
                  in_=ctx.dr["br0"], inc="sy", incn=16)
    ctx.wait("tensor", wr_m, br_m, st["stats_ready"])
    lg_done = None
    tr_done = None
    ev_done = None
    for tc in range(NT):
        b = 4 + tc % 2
        ctx.wait("vector", tr_done)
        ctx.do("vector", "tensor_scalar", S["ftmp"][:], xt(S, tc),
               S["lnst"][:, tc:tc + 1], S["lnst"][:, 16 + tc:17 + tc],
               ALU.subtract, ALU.mult, inc="dv")
        mx = ctx.tok("dv")
        ctx.wait("tensor", mx, ev_done, tr_done)
        for j in range(CC):
            ctx.do("tensor", "transpose",
                   bank(j // 4)[:, (j % 4) * 128:(j % 4 + 1) * 128],
                   S["ftmp"][:, j * 128:(j + 1) * 128], S["idnfs"][:],
                   inc="pe" if j == CC - 1 else None)
        mtr = ctx.tok("pe")
        ctx.wait("vector", mtr)
        ctx.do("vector", "tensor_copy", S["rt"][:, 0:512], bank(0))
        ctx.do("vector", "tensor_copy", S["rt"][:, 512:1024], bank(1),
               inc="dv")
        tr_done = ctx.tok("dv")
        ctx.wait("tensor", tr_done, lg_done)
        for cc in range(CC):
            ctx.do("tensor", "matmul", bank(b)[:, 0:E],
                   S["rt"][:, cc * 128:(cc + 1) * 128],
                   S["wrf"][:, cc * E:(cc + 1) * E],
                   start=(cc == 0), stop=False)
        ctx.do("tensor", "matmul", bank(b)[:, 0:E], S["one1f"][:],
               S["brf"][0:1, 0:E], start=False, stop=True, inc="pe")
        ev_done = ctx.tok("pe")
        ml = ctx.tok("pe")
        ctx.wait("vector", ml)
        ctx.do("vector", "reduce_max", S["sml"][:, 0:1], bank(b)[:, 0:E],
               AX.X, negate=True, inc="dv")
        ctx.wait("scalar", ctx.tok("dv"))
        ctx.do("scalar", "activation", S["ftmp"][:, 0:E], bank(b)[:, 0:E],
               AF.Exp, bias=S["sml"][:, 0:1], scale=1.0, inc="ac")
        lg_done = ctx.tok("ac")
        ctx.wait("vector", lg_done)
        ctx.do("vector", "max", S["sml"][:, 16:24], S["ftmp"][:, 0:E])
        ctx.do("vector", "drain")
        ctx.do("vector", "max_index", S["sml"][:, 24:32].bitcast(U32),
               S["sml"][:, 16:24], S["ftmp"][:, 0:E])
        ctx.do("vector", "drain")
        ctx.do("vector", "tensor_add", S["sml"][:, 1:2], S["sml"][:, 16:17],
               S["sml"][:, 17:18])
        ctx.do("vector", "tensor_copy", S["sml"][:, 4:6],
               S["sml"][:, 24:32].bitcast(U32)[:, 0:2])
        ctx.do("vector", "drain")
        ctx.do("vector", "reciprocal", S["sml"][:, 2:3], S["sml"][:, 1:2])
        ctx.do("vector", "drain")
        ctx.do("vector", "tensor_mul", S["sml"][:, 6:7], S["sml"][:, 16:17],
               S["sml"][:, 2:3])
        ctx.do("vector", "tensor_mul", S["sml"][:, 7:8], S["sml"][:, 17:18],
               S["sml"][:, 2:3])
        ctx.do("vector", "tensor_scalar", S["ftmp"][:, 8:16], S["io8"][:],
               S["sml"][:, 4:5], None, ALU.is_equal)
        ctx.do("vector", "tensor_scalar", S["ftmp"][:, 16:24], S["io8"][:],
               S["sml"][:, 5:6], None, ALU.is_equal)
        ctx.do("vector", "drain")
        ctx.do("vector", "tensor_scalar",
               S["call"][:, tc * E:(tc + 1) * E], S["ftmp"][:, 8:16],
               S["sml"][:, 6:7], None, ALU.mult)
        ctx.do("vector", "drain")
        ctx.do("vector", "scalar_tensor_tensor",
               S["call"][:, tc * E:(tc + 1) * E], S["ftmp"][:, 16:24],
               S["sml"][:, 7:8], S["call"][:, tc * E:(tc + 1) * E],
               ALU.mult, ALU.add, inc="dv")
    st["call_ready"] = ctx.tok("dv")
    st["rtr_pe"] = ctx.tok("pe")
    dump_buf(ctx, st, "c0_dbg", S["call"][:], st["call_ready"])


def emit_moe0(ctx, st):
    S = ctx.sb
    bank = ctx.bank
    dyn = S["dyn"]
    HB, WI, WG, WO = 0, 32768, 34816, 36864

    def dsl(off, n):
        return dyn[:, off:off + n]

    bi_m = ctx.do("sync", "dma_start", out=S["bib"][:], in_=ctx.dr["bi0"],
                  inc="sy", incn=16)
    bg_m = ctx.do("sync", "dma_start", out=S["bgh"][:], in_=ctx.dr["bg0"],
                  inc="sy", incn=16)
    ctx.wait("vector", bg_m)
    ctx.do("vector", "tensor_scalar", S["bgh"][:], S["bgh"][:], 0.5, None,
           ALU.mult, inc="dv")
    bgh_m = ctx.tok("dv")

    wi3 = ctx.dr["wi0"]
    wg3 = ctx.dr["wg0"]
    wo3 = ctx.dr["wo0"]
    ctx.wait("gpsimd", st.get("dyn_pe_free"), st.get("dyn_dv_free"))
    slot_free = {}
    bank_free = {}
    tmp_free = {}
    gb_free = None
    x_marks = []
    hseq = {}

    for e in range(MOE0_ESTART, MOE0_NE):
        if MOE0_BAR & 1:
            ctx.barrier()
        ctx.wait("sync", gb_free)
        gm = ctx.do("sync", "dma_start", out=S["gamb"][:],
                    in_=ctx.dr["gam0"][e], inc="sy", incn=16)
        bm = ctx.do("sync", "dma_start", out=S["betb"][:],
                    in_=ctx.dr["bet0"][e], inc="sy", incn=16)
        ctx.wait("gpsimd", st.get("bo_pe_free"))
        bo_m = dmaw(ctx, S["borow"][0:1, :], ctx.dr["bo0"][e:e + 1, :])
        for th in range(MOE0_NTH):
            if MOE0_BAR & 2:
                ctx.barrier()
            for fc in range(NF0):
                par = fc % 2
                ctx.wait("gpsimd", slot_free.get(("wi", par)))
                wim = dmaw(ctx, dsl(WI + par * 1024, 1024)
                           .rearrange("p (c m) -> p c m", c=CC),
                           wi3[e, fc])
                wgm = dmaw(ctx, dsl(WG + par * 1024, 1024)
                           .rearrange("p (c m) -> p c m", c=CC),
                           wg3[e, fc])
                for hf in range(2):
                    q = fc * 2 + hf
                    hb = (0, 2)[q % 2]
                    gb = (1, 3)[q % 2]
                    ctx.wait("tensor", wim, wgm, bank_free.get(hb),
                             bank_free.get(gb),
                             st["xn_ready"] if (e, th, fc, hf) == (0, 0, 0, 0)
                             else None,
                             bgh_m if (e, th, fc, hf) == (0, 0, 0, 0) else None)
                    for cc in range(CC):
                        ctx.do("tensor", "matmul", bank(hb),
                               dsl(WI + par * 1024 + cc * 128, 128),
                               xnf(S, cc, th * 1024 + hf * 512, 512),
                               start=(cc == 0), stop=(cc == CC - 1))
                    for cc in range(CC):
                        ctx.do("tensor", "matmul", bank(gb),
                               dsl(WG + par * 1024 + cc * 128, 128),
                               xnf(S, cc, th * 1024 + hf * 512, 512),
                               start=(cc == 0), stop=(cc == CC - 1),
                               inc="pe" if cc == CC - 1 else None)
                    mm_q = ctx.tok("pe")
                    if hf == 1:
                        slot_free[("wi", par)] = mm_q
                    tslot = q % 2
                    bcol = S["bib"][:, e * NF0 + fc:e * NF0 + fc + 1]
                    gcol = S["bgh"][:, e * NF0 + fc:e * NF0 + fc + 1]
                    ctx.wait("scalar", mm_q, tmp_free.get(tslot))
                    ctx.do("scalar", "activation",
                           S["tmp"][:, tslot * 512:(tslot + 1) * 512],
                           bank(hb), AF.Gelu, bias=bcol, scale=1.0)
                    ctx.do("scalar", "activation",
                           S["tmp"][:, 1024 + tslot * 512:1536 + tslot * 512],
                           bank(gb), AF.Tanh, bias=gcol, scale=0.5, inc="ac")
                    ev_q = ctx.tok("ac")
                    bank_free[hb] = ev_q
                    bank_free[gb] = ev_q
                    ctx.wait("vector", ev_q)
                    ctx.do("vector", "tensor_scalar",
                           S["tmp"][:, 1024 + tslot * 512:1536 + tslot * 512],
                           S["tmp"][:, 1024 + tslot * 512:1536 + tslot * 512],
                           0.5, 0.5, ALU.mult, ALU.add)
                    prev = hseq.get("prev")
                    if prev is not None:
                        pe_, pth_, pfc_, phf_ = prev
                        pq = pfc_ * 2 + phf_
                        pts = pq % 2
                        ctx.do("vector", "tensor_mul",
                               dsl(HB + pfc_ * 1024 + phf_ * 512, 512),
                               S["tmp"][:, pts * 512:(pts + 1) * 512],
                               S["tmp"][:, 1024 + pts * 512:1536 + pts * 512],
                               inc="dv")
                        tmp_free[pts] = ctx.tok("dv")
                    hseq["prev"] = (e, th, fc, hf)
            pe_, pth_, pfc_, phf_ = hseq["prev"]
            pq = pfc_ * 2 + phf_
            pts = pq % 2
            ctx.do("vector", "tensor_mul",
                   dsl(HB + pfc_ * 1024 + phf_ * 512, 512),
                   S["tmp"][:, pts * 512:(pts + 1) * 512],
                   S["tmp"][:, 1024 + pts * 512:1536 + pts * 512], inc="dv")
            tmp_free[pts] = ctx.tok("dv")
            hseq["prev"] = None
            h_ready = ctx.tok("dv")
            if MOE0_BAR & 8:
                ctx.barrier()

            for grp in range(MOE0_NGRP):
                if MOE0_BAR & 4:
                    ctx.barrier()
                grp_marks = {}
                ctx.wait("tensor", h_ready,
                         *[bank_free.get(b) for b in range(8)])
                for j in range(4):
                    tc8 = grp * 4 + j
                    for half in range(2):
                        b = 2 * j + half
                        for ci in range(4):
                            cc = half * 4 + ci
                            ctx.do("tensor", "matmul",
                                   bank(b)[:, ci * 128:(ci + 1) * 128],
                                   xnf(S, cc, th * 1024 + tc8 * 128, 128),
                                   S["cst"][:],
                                   start=(ci == 0), stop=False)
                for fc in range(NF0):
                    par = fc % 2
                    ctx.wait("gpsimd", slot_free.get(("wo", par)))
                    wom = dmaw(ctx, dsl(WO + par * 1024, 1024), wo3[e, fc])
                    ctx.wait("tensor", wom)
                    for j in range(4):
                        tc8 = grp * 4 + j
                        for half in range(2):
                            b = 2 * j + half
                            ctx.do("tensor", "matmul", bank(b),
                                   dsl(HB + fc * 1024 + tc8 * 128, 128),
                                   dsl(WO + par * 1024 + half * 512, 512),
                                   start=False, stop=False,
                                   inc="pe" if (j, half) == (3, 1) else None)
                    slot_free[("wo", par)] = ctx.tok("pe")
                ctx.wait("tensor", bo_m)
                for j in range(4):
                    for half in range(2):
                        b = 2 * j + half
                        ctx.do("tensor", "matmul", bank(b), S["one1"][:],
                               S["borow"][0:1, half * 512:(half + 1) * 512],
                               start=False, stop=True,
                               inc="pe" if half == 1 else None)
                    grp_marks[j] = ctx.tok("pe")
                for j in range(4):
                    tc8 = grp * 4 + j
                    tc = th * 8 + tc8
                    ctx.wait("vector", grp_marks[j], gm, bm,
                             st["call_ready"] if (e, th, grp, j) == (0, 0, 0, 0)
                             else None)
                    for half in range(2):
                        b = 2 * j + half
                        ctx.do("vector", "tensor_copy",
                               S["rt"][:, half * 512:(half + 1) * 512],
                               bank(b), inc="dv" if half == 1 else None)
                    bank_free[2 * j] = ctx.tok("dv")
                    bank_free[2 * j + 1] = ctx.tok("dv")
                    ctx.do("vector", "drain")
                    ctx.do("vector", "bn_stats", S["sml"][:, 0:6],
                           S["rt"][:, 0:512])
                    ctx.do("vector", "bn_stats", S["sml"][:, 6:12],
                           S["rt"][:, 512:1024])
                    ctx.do("vector", "drain")
                    ctx.do("vector", "bn_aggr", S["sml"][:, 12:14],
                           S["sml"][:, 0:12], inc="dv")
                    ctx.wait("scalar", ctx.tok("dv"))
                    ctx.do("scalar", "activation", S["sml"][:, 14:15],
                           S["sml"][:, 13:14], AF.Sqrt,
                           bias=S["epss"][:, 1:2], scale=1.0, inc="ac")
                    ctx.wait("vector", ctx.tok("ac"))
                    ctx.do("vector", "reciprocal", S["sml"][:, 15:16],
                           S["sml"][:, 14:15])
                    ctx.do("vector", "drain")
                    ctx.do("vector", "tensor_mul", S["sml"][:, 16:17],
                           S["call"][:, tc * E + e:tc * E + e + 1],
                           S["sml"][:, 15:16])
                    ctx.do("vector", "drain")
                    ctx.do("vector", "tensor_mul", S["sml"][:, 17:18],
                           S["sml"][:, 12:13], S["sml"][:, 16:17])
                    ctx.do("vector", "drain")
                    ctx.do("vector", "tensor_scalar", S["sml"][:, 17:18],
                           S["sml"][:, 17:18], -1.0, None, ALU.mult)
                    ctx.do("vector", "drain")
                    ctx.do("vector", "tensor_scalar", S["ftmp"][:],
                           S["rt"][:], S["sml"][:, 16:17], S["sml"][:, 17:18],
                           ALU.mult, ALU.add)
                    ctx.do("vector", "drain")
                    ctx.do("vector", "tensor_mul", S["ftmp"][:],
                           S["ftmp"][:], S["gamb"][:])
                    ctx.do("vector", "drain")
                    ctx.do("vector", "scalar_tensor_tensor", S["ftmp"][:],
                           S["betb"][:],
                           S["call"][:, tc * E + e:tc * E + e + 1],
                           S["ftmp"][:], ALU.mult, ALU.add)
                    ctx.do("vector", "drain")
                    ctx.do("vector", "tensor_add", xt(S, tc), xt(S, tc),
                           S["ftmp"][:], inc="dv")
                    x_marks.append(ctx.tok("dv"))
        gb_free = ctx.tok("dv")
        st["bo_pe_free"] = ctx.tok("pe")
    st["x_writes"] = [x_marks[-1]]
    st["xn_consumers"].append(ctx.tok("pe"))
    st["dyn_pe_free"] = ctx.tok("pe")
    dump_buf(ctx, st, "x_m0", S["x"][:], x_marks[-1])
    dump_buf(ctx, st, "h_dbg", S["dyn"][:, 0:32768], x_marks[-1])
    dump_buf(ctx, st, "r_dbg", S["rt"][:], x_marks[-1])





def emit_router1(ctx, st):
    S = ctx.sb
    bank = ctx.bank
    gw_m = ctx.do("sync", "dma_start",
                  out=S["wrf"][:].rearrange("p (c e) -> p c e", e=E),
                  in_=ctx.dr["gw1"].rearrange("c p e -> p c e"),
                  inc="sy", incn=16)
    ctx.wait("tensor", gw_m, st["stats_ready"])
    lg_done = None
    tr_done = None
    ev_done = None
    cnt_done = None
    for tc in range(NT):
        b = 4 + tc % 2
        ctx.wait("vector", tr_done)
        ctx.do("vector", "tensor_scalar", S["ftmp"][:], xt(S, tc),
               S["lnst"][:, tc:tc + 1], S["lnst"][:, 16 + tc:17 + tc],
               ALU.subtract, ALU.mult, inc="dv")
        mx = ctx.tok("dv")
        ctx.wait("tensor", mx, ev_done, tr_done)
        for j in range(CC):
            ctx.do("tensor", "transpose",
                   bank(j // 4)[:, (j % 4) * 128:(j % 4 + 1) * 128],
                   S["ftmp"][:, j * 128:(j + 1) * 128], S["idnfs"][:],
                   inc="pe" if j == CC - 1 else None)
        mtr = ctx.tok("pe")
        ctx.wait("vector", mtr)
        ctx.do("vector", "tensor_copy", S["rt"][:, 0:512], bank(0))
        ctx.do("vector", "tensor_copy", S["rt"][:, 512:1024], bank(1),
               inc="dv")
        tr_done = ctx.tok("dv")
        ctx.wait("tensor", tr_done, lg_done)
        for cc in range(CC):
            ctx.do("tensor", "matmul", bank(b)[:, 0:E],
                   S["rt"][:, cc * 128:(cc + 1) * 128],
                   S["wrf"][:, cc * E:(cc + 1) * E],
                   start=(cc == 0), stop=(cc == CC - 1), inc="pe" if cc == CC - 1 else None)
        ev_done = ctx.tok("pe")
        ml = ev_done
        ctx.wait("vector", ml)
        ctx.do("vector", "reduce_max", S["sml"][:, 0:1], bank(b)[:, 0:E],
               AX.X, negate=True, inc="dv")
        ctx.wait("scalar", ctx.tok("dv"))
        ctx.do("scalar", "activation", S["ftmp"][:, 0:E], bank(b)[:, 0:E],
               AF.Exp, bias=S["sml"][:, 0:1], scale=1.0,
               accum_out=S["sml"][:, 3:4], inc="ac")
        lg_done = ctx.tok("ac")
        ctx.wait("vector", lg_done)
        ctx.do("vector", "max", S["sml"][:, 16:24], S["ftmp"][:, 0:E])
        ctx.do("vector", "reciprocal", S["sml"][:, 2:3], S["sml"][:, 3:4])
        ctx.do("vector", "max_index", S["sml"][:, 24:32].bitcast(U32),
               S["sml"][:, 16:24], S["ftmp"][:, 0:E])
        ctx.do("vector", "drain")
        ctx.do("vector", "tensor_scalar", S["sml"][:, 8:10], S["sml"][:, 16:18],
               S["sml"][:, 2:3], None, ALU.mult)
        ctx.do("vector", "tensor_copy", S["sml"][:, 4:6],
               S["sml"][:, 24:32].bitcast(U32)[:, 0:2])
        ctx.do("vector", "drain")
        ctx.do("vector", "tensor_add", S["sml"][:, 1:2], S["sml"][:, 8:9],
               S["sml"][:, 9:10])
        ctx.do("vector", "drain")
        ctx.do("vector", "tensor_scalar", S["sml"][:, 1:2], S["sml"][:, 1:2],
               1e-9, None, ALU.add)
        ctx.do("vector", "drain")
        ctx.do("vector", "reciprocal", S["sml"][:, 2:3], S["sml"][:, 1:2])
        ctx.do("vector", "drain")
        ctx.do("vector", "tensor_scalar", S["wn12"][:, tc * 2:tc * 2 + 2],
               S["sml"][:, 8:10], S["sml"][:, 2:3], None, ALU.mult)
        ctx.do("vector", "tensor_scalar", S["m12"][:, tc * 16:tc * 16 + 8],
               S["io8"][:], S["sml"][:, 4:5], None, ALU.is_equal)
        ctx.do("vector", "tensor_scalar", S["m12"][:, tc * 16 + 8:tc * 16 + 16],
               S["io8"][:], S["sml"][:, 5:6], None, ALU.is_equal, inc="dv")
        mm12 = ctx.tok("dv")
        ctx.wait("tensor", mm12, cnt_done if tc == 0 else None)
        ctx.do("tensor", "matmul", bank(2)[0:1, 0:16], S["onef"][:],
               S["m12"][:, tc * 16:(tc + 1) * 16],
               start=(tc == 0), stop=(tc == NT - 1),
               inc="pe" if tc == NT - 1 else None)
    mcnt = ctx.tok("pe")
    ctx.wait("vector", mcnt)
    ctx.do("vector", "tensor_copy", S["carry"][:], bank(2)[0:1, 0:16],
           inc="dv")
    ctx.wait("sync", ctx.tok("dv"))
    ctx.do("sync", "dma_start", out=ctx.dr["r_in"], in_=S["carry"][:],
           inc="sy", incn=16)
    ctx.wait("gpsimd", ctx.tok("sy"))
    ctx.do("gpsimd", "collective_compute", "AllGather", ALU.bypass,
           replica_groups=GROUPS_ALL,
           ins=[ctx.dr["r_in"]], outs=[ctx.dr["r_out"]], inc="c2")
    ctx.wait("sync", ctx.tok("c2"))
    ctx.do("sync", "dma_start", out=S["rowt"][:], in_=ctx.dr["r_out"],
           inc="sy", incn=16)
    rw_m = ctx.tok("sy")
    ctx.wait("tensor", rw_m)
    ctx.do("tensor", "matmul", bank(3)[:, 0:16], S["omask"][:], S["rowt"][:],
           start=True, stop=True, inc="pe")
    moff = ctx.tok("pe")
    ctx.wait("vector", moff)
    ctx.do("vector", "tensor_copy", S["cnt16"][:], bank(3)[:, 0:16])
    ctx.do("vector", "drain")
    ctx.do("vector", "tensor_copy", S["carry"][:], S["cnt16"][0:1, :],
           inc="dv")
    carry_m = ctx.tok("dv")

    rank_done = None
    for tc in range(NT):
        b = 2 + tc % 2
        ctx.wait("tensor", carry_m if tc == 0 else None, rank_done)
        ctx.do("tensor", "matmul", bank(b)[:, 0:16], S["linc"][:],
               S["m12"][:, tc * 16:(tc + 1) * 16], start=True, stop=False)
        ctx.do("tensor", "matmul", bank(b)[:, 0:16], S["one1f"][:],
               S["carry"][:], start=False, stop=True)
        ctx.do("tensor", "matmul", bank(b)[0:1, 16:32], S["onef"][:],
               S["m12"][:, tc * 16:(tc + 1) * 16],
               start=True, stop=True, inc="pe")
        mrank = ctx.tok("pe")
        ctx.wait("vector", mrank)
        ctx.do("vector", "tensor_add", S["carry"][:], S["carry"][:],
               bank(b)[0:1, 16:32])
        ctx.do("vector", "tensor_mul", S["ftmp"][:, 0:16],
               bank(b)[:, 0:16], S["m12"][:, tc * 16:(tc + 1) * 16],
               inc="dv")
        rank_done = ctx.tok("dv")
        ctx.do("vector", "drain")
        ctx.do("vector", "reduce_sum", S["sml"][:, 0:1], S["ftmp"][:, 0:8],
               AX.X)
        ctx.do("vector", "reduce_sum", S["sml"][:, 1:2], S["ftmp"][:, 8:16],
               AX.X)
        ctx.do("vector", "drain")
        ctx.do("vector", "tensor_scalar", S["sml"][:, 2:4], S["sml"][:, 0:2],
               CAP, None, ALU.is_le)
        ctx.do("vector", "drain")
        ctx.do("vector", "tensor_mul", S["sml"][:, 4:6],
               S["wn12"][:, tc * 2:tc * 2 + 2], S["sml"][:, 2:4])
        ctx.do("vector", "drain")
        ctx.do("vector", "tensor_scalar",
               S["c1al"][:, tc * E:(tc + 1) * E],
               S["m12"][:, tc * 16:tc * 16 + 8],
               S["sml"][:, 4:5], None, ALU.mult)
        ctx.do("vector", "drain")
        ctx.do("vector", "scalar_tensor_tensor",
               S["c1al"][:, tc * E:(tc + 1) * E],
               S["m12"][:, tc * 16 + 8:tc * 16 + 16],
               S["sml"][:, 5:6], S["c1al"][:, tc * E:(tc + 1) * E],
               ALU.mult, ALU.add, inc="dv")
    st["c1_ready"] = ctx.tok("dv")
    dump_buf(ctx, st, "c1_dbg", S["c1al"][:], st["c1_ready"])



def phi_evac(ctx, psum_ap, bias_col, out_bf, fbuf):
    S = ctx.sb
    ctx.do("vector", "tensor_scalar", fbuf[:, 0:512], psum_ap,
           bias_col, 0.0, ALU.add, ALU.min, inc="dv")
    ctx.wait("scalar", ctx.tok("dv"))
    ctx.do("scalar", "activation", S["rt"][:, 0:512], fbuf[:, 0:512],
           AF.Exp, inc="ac")
    ctx.do("vector", "tensor_scalar", fbuf[:, 512:1024], psum_ap,
           bias_col, 0.0, ALU.add, ALU.max)
    ctx.wait("vector", ctx.tok("ac"))
    ctx.do("vector", "tensor_add", out_bf, fbuf[:, 512:1024],
           S["rt"][:, 0:512], inc="dv")
    return ctx.tok("dv")


def emit_attn1(ctx, st):
    S = ctx.sb
    bank, bankb = ctx.bank, ctx.bankb
    dyn = S["dyn"]

    def dsl(off, n):
        return dyn[:, off:off + n]

    QG, KG, VG = 0, 4096, 8192
    WQS, WKS, WVS, WOS = 12288, 14336, 16384, 24576
    KTM = 32768
    ATM = 33024
    AFM = 35072
    ctx.wait("gpsimd", st.get("dyn_pe_free"), st.get("dyn_dv_free"))
    bq_m = ctx.do("sync", "dma_start", out=S["bsm"][:, 0:CC],
                  in_=ctx.dr["bq1"], inc="sy", incn=16)
    bk_m = ctx.do("sync", "dma_start", out=S["bsm"][:, CC:2 * CC],
                  in_=ctx.dr["bk1"], inc="sy", incn=16)
    bv_m = dmaw(ctx, S["bvrow"][0:1, :], ctx.dr["bv1row"])
    bo_m = dmaw(ctx, S["borow"][0:1, :], ctx.dr["bo1"])
    wv_m = dmaw(ctx, dsl(WVS, 8192).rearrange("p (c m) -> p c m", c=CC),
                ctx.dr["wv1"].rearrange("c p m -> p c m"))
    ctx.wait("vector", st.get("Ssb_free"), bq_m, bk_m)
    ctx.do("vector", "memset", S["Ssb"][:], 0.0, inc="dv")
    s_zero = ctx.tok("dv")

    xn_rdy = st["xn_ready"]
    prod_banks = [0, 1]
    pd = {}

    def produce_group(g, what, phi, wslot, wdram, bias_idx, dst):
        if what == "fm":
            for oc in range(CC):
                par = oc % 2
                ctx.wait("gpsimd", pd.get(("ws", wslot, par)))
                wm = dmaw(ctx, dsl(wslot + par * 1024, 1024)
                          .rearrange("p (c m) -> p c m", c=CC),
                          wdram[oc])
                b = prod_banks[oc % 2]
                ctx.wait("tensor", wm, xn_rdy if g == 0 and oc == 0 else None,
                         pd.get(("pb", b)))
                for cc in range(CC):
                    ctx.do("tensor", "matmul", bank(b),
                           dsl(wslot + par * 1024 + cc * 128, 128),
                           xnf(S, cc, g * 512, 512),
                           start=(cc == 0), stop=(cc == CC - 1),
                           inc="pe" if cc == CC - 1 else None)
                mq = ctx.tok("pe")
                pd[("ws", wslot, par)] = mq
                ctx.wait("vector", mq)
                m = phi_evac(ctx, bank(b),
                             S["bsm"][:, bias_idx * CC + oc:bias_idx * CC + oc + 1],
                             dsl(dst + oc * 512, 512), S["ftmp"])
                pd[("pb", b)] = m
            return m
        else:
            for i in range(4):
                for half in range(2):
                    b = prod_banks[(i * 2 + half) % 2]
                    ctx.wait("tensor", wv_m, bv_m, pd.get(("pb", b)),
                             xn_rdy if g == 0 and i == 0 else None)
                    for cc in range(CC):
                        ctx.do("tensor", "matmul",
                               bank(b)[:, 0:512],
                               xnf(S, cc, g * 512 + i * 128, 128),
                               dsl(WVS + cc * 1024 + half * 512, 512),
                               start=(cc == 0), stop=False)
                    ctx.do("tensor", "matmul", bank(b)[:, 0:512], S["one1"][:],
                           S["bvrow"][0:1, half * 512:(half + 1) * 512],
                           start=False, stop=True,
                           inc="pe")
                    mq = ctx.tok("pe")
                    ctx.wait("vector", mq)
                    ctx.do("vector", "tensor_copy",
                           dsl(VG + i * 1024 + half * 512, 512),
                           bank(b)[:, 0:512], inc="dv")
                    pd[("pb", b)] = ctx.tok("dv")
            return ctx.tok("dv")

    tr_par = [None, None]
    sd_done = {}
    for g in range(4):
        produce_group(g, "fm", True, WKS, ctx.dr["wk1"], 1, KG)
        kv_rdy = produce_group(g, "tm", False, None, None, None, VG)
        ctx.wait("tensor", kv_rdy)
        for ci in range(4):
            for h in range(16):
                b64 = (h % 2) * 64
                par = h % 2
                pt = bankb(6 + par)
                ctx.wait("tensor", tr_par[par])
                ctx.do("tensor", "transpose", pt[:, 0:64],
                       dyn[b64:b64 + 64,
                           KG + (h // 2) * 512 + ci * 128:
                           KG + (h // 2) * 512 + ci * 128 + 128],
                       S["cst"][b64:b64 + 64, b64:b64 + 64],
                       tile_position=(b64, 0), inc="pe")
                mt = ctx.tok("pe")
                ctx.wait("vector", mt)
                ctx.do("vector", "tensor_copy",
                       dsl(KTM + par * 128, 64), pt[:, 0:64], inc="dv")
                tr_par[par] = ctx.tok("dv")
                bsd = 2 + (h % 2)
                ctx.wait("tensor", tr_par[par], sd_done.get(bsd))
                ctx.do("tensor", "matmul",
                       bank(bsd)[b64:b64 + 64, 0:64],
                       dsl(KTM + par * 128, 64),
                       dsl(VG + ci * 1024 + h * 64, 64),
                       start=True, stop=True, tile_position=(0, b64))
                ctx.do("tensor", "matmul",
                       bank(bsd)[b64:b64 + 64, 64:65],
                       dsl(KTM + par * 128, 64), S["onec"][:],
                       start=True, stop=True, tile_position=(0, b64),
                       inc="pe")
                msd = ctx.tok("pe")
                ctx.wait("vector", msd)
                ctx.do("vector", "tensor_add",
                       S["Ssb"][b64:b64 + 64, (h // 2) * 64:(h // 2) * 64 + 64],
                       S["Ssb"][b64:b64 + 64, (h // 2) * 64:(h // 2) * 64 + 64],
                       bank(bsd)[b64:b64 + 64, 0:64])
                ctx.do("vector", "tensor_add",
                       S["Ssb"][b64:b64 + 64, 512 + h // 2:513 + h // 2],
                       S["Ssb"][b64:b64 + 64, 512 + h // 2:513 + h // 2],
                       bank(bsd)[b64:b64 + 64, 64:65], inc="dv")
                sd_done[bsd] = ctx.tok("dv")
    s_local = ctx.tok("dv")
    dump_buf(ctx, st, "st_dbg", S["Ssb"][:], s_local)

    ctx.wait("sync", s_local, s_zero)
    ctx.do("sync", "dma_start", out=ctx.dr["s_in"], in_=S["Ssb"][:],
           inc="sy", incn=16)
    ctx.wait("gpsimd", ctx.tok("sy"))
    ctx.do("gpsimd", "collective_compute", "AllGather", ALU.bypass,
           replica_groups=GROUPS_ALL,
           ins=[ctx.dr["s_in"]], outs=[ctx.dr["s_out"]], inc="c1")
    cm_ = ctx.tok("c1")
    ctx.wait("vector", cm_)
    ctx.do("vector", "memset", S["Ssb"][:], 0.0, inc="dv")
    so = ctx.dr["s_out"].rearrange("(r p) c -> r p c", p=128)
    for r in range(N_CORES):
        lm = ctx.do("sync", "dma_start", out=S["gath"][:], in_=so[r],
                    waits=[cm_], inc="sy", incn=16)
        ctx.wait("vector", lm)
        ctx.do("vector", "scalar_tensor_tensor", S["Ssb"][:], S["gath"][:],
               S["pmask"][:, r:r + 1], S["Ssb"][:], ALU.mult, ALU.add,
               inc="dv")
        ctx.do("vector", "drain")
        ctx.wait("sync", ctx.tok("dv"))
    ctx.do("vector", "tensor_copy", S["Sbf"][:], S["Ssb"][:], inc="dv")
    s_init = ctx.tok("dv")

    wo_m = None
    x_marks = []
    ob = {}
    ctx.wait("tensor", s_init)
    for g in range(4):
        produce_group(g, "fm", True, WQS, ctx.dr["wq1"], 0, QG)
        produce_group(g, "fm", True, WKS, ctx.dr["wk1"], 1, KG)
        kv_rdy = produce_group(g, "tm", False, None, None, None, VG)
        if g == 0:
            ctx.wait("gpsimd", ctx.tok("pe"))
            wo_m = dmaw(ctx, dsl(WOS, 8192).rearrange("p (c m) -> p c m", c=CC),
                        ctx.dr["wo1"].rearrange("c p m -> p c m"))
        for ci in range(4):
            tc = g * 4 + ci
            par = tc % 2
            atm = dsl(ATM + par * 1024, 1024)
            afm = dsl(AFM + par * 1024, 1024)
            for h in range(16):
                b64 = (h % 2) * 64
                hb2 = h // 2
                qsl = dyn[b64:b64 + 64,
                          QG + hb2 * 512 + ci * 128:QG + hb2 * 512 + ci * 128 + 128]
                ksl = dyn[b64:b64 + 64,
                          KG + hb2 * 512 + ci * 128:KG + hb2 * 512 + ci * 128 + 128]
                vsl = dsl(VG + ci * 1024 + h * 64, 64)
                tb = (h % 2) * 256
                bsT = 2 + (h % 2)
                bs = 4 + (h % 2)
                ctx.wait("tensor", kv_rdy if h == 0 else None,
                         ob.get(("sT", bsT)), ob.get(("s", bs)),
                         sd_done.get(bsT), ob.get(("sbf",)) if h == 0 else None)
                ctx.do("tensor", "matmul", bank(bsT)[:, 0:128], ksl, qsl,
                       start=True, stop=True, tile_position=(b64, 0))
                ctx.do("tensor", "matmul", bank(bsT)[:, 128:256], qsl, ksl,
                       start=True, stop=True, tile_position=(b64, 0))
                ctx.do("tensor", "matmul", bank(bsT)[:, 256:257], qsl,
                       S["Sbf"][b64:b64 + 64, 512 + hb2:513 + hb2],
                       start=True, stop=True, tile_position=(b64, 0),
                       inc="pe")
                msc = ctx.tok("pe")
                ctx.wait("vector", msc, ob.get(("tmp", h % 2)))
                ctx.do("vector", "tensor_mul", S["tmp"][:, tb:tb + 128],
                       bank(bsT)[:, 0:128], S["mkq"][:])
                ctx.do("vector", "tensor_mul", S["tmp"][:, tb + 128:tb + 256],
                       bank(bsT)[:, 128:256], S["mqk"][:], inc="dv")
                ob[("sT", bsT)] = ctx.tok("dv")
                ctx.do("vector", "drain")
                ctx.do("vector", "reduce_sum", S["sml"][:, 0:1],
                       S["tmp"][:, tb + 128:tb + 256], AX.X)
                ctx.do("vector", "drain")
                ctx.do("vector", "tensor_add", S["sml"][:, 1:2],
                       S["sml"][:, 0:1], bank(bsT)[:, 256:257])
                ctx.do("vector", "drain")
                ctx.do("vector", "reciprocal", S["sml"][:, 2:3],
                       S["sml"][:, 1:2], inc="dv")
                mden = ctx.tok("dv")
                ctx.wait("tensor", mden)
                ctx.do("tensor", "matmul", bank(bs)[:, 0:64],
                       S["tmp"][:, tb:tb + 128], vsl, start=True, stop=False)
                ctx.do("tensor", "matmul", bank(bs)[:, 0:64], qsl,
                       S["Sbf"][b64:b64 + 64, hb2 * 64:hb2 * 64 + 64],
                       start=False, stop=True, tile_position=(b64, 0),
                       inc="pe")
                mnum = ctx.tok("pe")
                ctx.wait("vector", mnum)
                ctx.do("vector", "tensor_scalar", atm[:, h * 64:h * 64 + 64],
                       bank(bs)[:, 0:64], S["sml"][:, 2:3], None, ALU.mult,
                       inc="dv")
                ob[("s", bs)] = ctx.tok("dv")
                ob[("tmp", h % 2)] = mnum
            ctx.wait("tensor", ctx.tok("dv"))
            for h in range(16):
                b64 = (h % 2) * 64
                hb2 = h // 2
                parh = h % 2
                pt = bankb(6 + parh)
                ctx.wait("tensor", tr_par[parh])
                ctx.do("tensor", "transpose", pt[:, 0:64],
                       dyn[b64:b64 + 64,
                           KG + hb2 * 512 + ci * 128:KG + hb2 * 512 + ci * 128 + 128],
                       S["cst"][b64:b64 + 64, b64:b64 + 64],
                       tile_position=(b64, 0), inc="pe")
                mt = ctx.tok("pe")
                ctx.wait("vector", mt)
                ctx.do("vector", "tensor_copy",
                       dsl(KTM + parh * 128, 64), pt[:, 0:64], inc="dv")
                tr_par[parh] = ctx.tok("dv")
                bsd = 2 + parh
                ctx.wait("tensor", tr_par[parh], sd_done.get(bsd))
                ctx.do("tensor", "matmul",
                       bank(bsd)[b64:b64 + 64, 300:364],
                       dsl(KTM + parh * 128, 64),
                       dsl(VG + ci * 1024 + h * 64, 64),
                       start=True, stop=True, tile_position=(0, b64))
                ctx.do("tensor", "matmul",
                       bank(bsd)[b64:b64 + 64, 364:365],
                       dsl(KTM + parh * 128, 64), S["onec"][:],
                       start=True, stop=True, tile_position=(0, b64),
                       inc="pe")
                msd = ctx.tok("pe")
                ctx.wait("vector", msd)
                ctx.do("vector", "tensor_add",
                       S["Ssb"][b64:b64 + 64, hb2 * 64:hb2 * 64 + 64],
                       S["Ssb"][b64:b64 + 64, hb2 * 64:hb2 * 64 + 64],
                       bank(bsd)[b64:b64 + 64, 300:364])
                ctx.do("vector", "tensor_add",
                       S["Ssb"][b64:b64 + 64, 512 + hb2:513 + hb2],
                       S["Ssb"][b64:b64 + 64, 512 + hb2:513 + hb2],
                       bank(bsd)[b64:b64 + 64, 364:365], inc="dv")
                sd_done[bsd] = ctx.tok("dv")
            ctx.do("vector", "drain")
            ctx.do("vector", "tensor_copy", S["Sbf"][:], S["Ssb"][:],
                   inc="dv")
            sbf_m = ctx.tok("dv")
            ob[("sbf",)] = sbf_m
            matm = ctx.tok("dv")
            pt = bankb(6 + par)
            ctx.wait("tensor", matm, tr_par[par])
            for j in range(CC):
                ctx.do("tensor", "transpose", pt[:, j * 128:(j + 1) * 128],
                       atm[:, j * 128:(j + 1) * 128], S["cst"][:],
                       inc="pe" if j == CC - 1 else None)
            mtr = ctx.tok("pe")
            tr_par[par] = None
            ctx.wait("vector", mtr)
            for j in range(CC):
                ctx.do("vector", "tensor_copy", afm[:, j * 128:(j + 1) * 128],
                       pt[:, j * 128:(j + 1) * 128],
                       inc="dv" if j == CC - 1 else None)
            maf = ctx.tok("dv")
            tr_par[par] = maf
            ctx.wait("tensor", maf, wo_m, bo_m, ob.get(("proj",)))
            for half in range(2):
                bb = (0, 1)[half]
                ctx.wait("tensor", pd.get(("pb", bb)))
                for j in range(CC):
                    ctx.do("tensor", "matmul", bank(bb),
                           afm[:, j * 128:(j + 1) * 128],
                           dsl(WOS + j * 1024 + half * 512, 512),
                           start=(j == 0), stop=False)
                ctx.do("tensor", "matmul", bank(bb), S["one1"][:],
                       S["borow"][0:1, half * 512:(half + 1) * 512],
                       start=False, stop=True, inc="pe")
            mpr = ctx.tok("pe")
            ctx.wait("vector", mpr)
            ctx.do("vector", "tensor_add", xt(S, tc)[:, 0:512],
                   xt(S, tc)[:, 0:512], bank(0))
            ctx.do("vector", "tensor_add", xt(S, tc)[:, 512:1024],
                   xt(S, tc)[:, 512:1024], bank(1), inc="dv")
            mx_ = ctx.tok("dv")
            pd[("pb", 0)] = mx_
            pd[("pb", 1)] = mx_
            ob[("proj",)] = mx_
            x_marks.append(mx_)
    st["x_writes"] = [x_marks[-1]]
    st["xn_consumers"].append(ctx.tok("pe"))
    st["dyn_pe_free"] = ctx.tok("pe")
    st["dyn_dv_free"] = ctx.tok("dv")
    st["bo_pe_free"] = ctx.tok("pe")
    st["Ssb_free"] = ctx.tok("dv")
    dump_buf(ctx, st, "x_a1", S["x"][:], x_marks[-1])



def emit_moe1(ctx, st):
    S = ctx.sb
    bank = ctx.bank
    dyn = S["dyn"]
    GU, WG1, WU1, WD1 = 0, 32768, 34816, 36864

    def dsl(off, n):
        return dyn[:, off:off + n]

    ctx.wait("gpsimd", st.get("dyn_pe_free"), st.get("dyn_dv_free"))
    wg3 = ctx.dr["wg1"]
    wu3 = ctx.dr["wu1"]
    wd3 = ctx.dr["wd1"]
    slot_free = {}
    bank_free = {}
    tmp_free = {}
    x_marks = []
    prev = [None]

    for e in range(E):
        for hc in range(NH1):
            par = hc % 2
            ctx.wait("gpsimd", slot_free.get(("wg", par)))
            wgm = dmaw(ctx, dsl(WG1 + par * 1024, 1024)
                       .rearrange("p (c m) -> p c m", c=CC), wg3[e, hc])
            wum = dmaw(ctx, dsl(WU1 + par * 1024, 1024)
                       .rearrange("p (c m) -> p c m", c=CC), wu3[e, hc])
            for tg in range(4):
                q = hc * 4 + tg
                gb = (0, 2)[q % 2]
                ub = (1, 3)[q % 2]
                ctx.wait("tensor", wgm, wum, bank_free.get(gb),
                         bank_free.get(ub),
                         st["xn_ready"] if (e, hc, tg) == (0, 0, 0) else None,
                         st.get("c1_ready") if (e, hc, tg) == (0, 0, 0) else None)
                for cc in range(CC):
                    ctx.do("tensor", "matmul", bank(gb),
                           dsl(WG1 + par * 1024 + cc * 128, 128),
                           xnf(S, cc, tg * 512, 512),
                           start=(cc == 0), stop=(cc == CC - 1))
                for cc in range(CC):
                    ctx.do("tensor", "matmul", bank(ub),
                           dsl(WU1 + par * 1024 + cc * 128, 128),
                           xnf(S, cc, tg * 512, 512),
                           start=(cc == 0), stop=(cc == CC - 1),
                           inc="pe" if cc == CC - 1 else None)
                mm_q = ctx.tok("pe")
                if tg == 3:
                    slot_free[("wg", par)] = mm_q
                ts_ = q % 2
                ctx.wait("scalar", mm_q, tmp_free.get(ts_))
                ctx.do("scalar", "activation",
                       S["tmp"][:, ts_ * 512:(ts_ + 1) * 512],
                       bank(gb), AF.Silu)
                ctx.do("scalar", "activation",
                       S["tmp"][:, 1024 + ts_ * 512:1536 + ts_ * 512],
                       bank(ub), AF.Copy, inc="ac")
                ev_q = ctx.tok("ac")
                bank_free[gb] = ev_q
                bank_free[ub] = ev_q
                ctx.wait("vector", ev_q)
                if prev[0] is not None:
                    phc, ptg = prev[0]
                    pts = (phc * 4 + ptg) % 2
                    ctx.do("vector", "tensor_mul",
                           dsl(GU + phc * 2048 + ptg * 512, 512),
                           S["tmp"][:, pts * 512:(pts + 1) * 512],
                           S["tmp"][:, 1024 + pts * 512:1536 + pts * 512],
                           inc="dv")
                    tmp_free[pts] = ctx.tok("dv")
                prev[0] = (hc, tg)
        phc, ptg = prev[0]
        pts = (phc * 4 + ptg) % 2
        ctx.do("vector", "tensor_mul",
               dsl(GU + phc * 2048 + ptg * 512, 512),
               S["tmp"][:, pts * 512:(pts + 1) * 512],
               S["tmp"][:, 1024 + pts * 512:1536 + pts * 512], inc="dv")
        tmp_free[pts] = ctx.tok("dv")
        prev[0] = None
        gu_ready = ctx.tok("dv")

        for grp in range(4):
            grp_marks = {}
            ctx.wait("tensor", gu_ready,
                     *[bank_free.get(b) for b in range(8)])
            for hc in range(NH1):
                par = hc % 2
                ctx.wait("gpsimd", slot_free.get(("wd", par)))
                wdm = dmaw(ctx, dsl(WD1 + par * 1024, 1024), wd3[e, hc])
                ctx.wait("tensor", wdm)
                for j in range(4):
                    tc = grp * 4 + j
                    for half in range(2):
                        b = 2 * j + half
                        ctx.do("tensor", "matmul", bank(b),
                               dsl(GU + hc * 2048 + tc * 128, 128),
                               dsl(WD1 + par * 1024 + half * 512, 512),
                               start=(hc == 0), stop=(hc == NH1 - 1),
                               inc="pe" if (j, half) == (3, 1) else None)
                slot_free[("wd", par)] = ctx.tok("pe")
                if hc == NH1 - 1:
                    for j in range(4):
                        grp_marks[j] = ctx.tok("pe")
            for j in range(4):
                tc = grp * 4 + j
                ctx.wait("vector", grp_marks[j])
                for half in range(2):
                    b = 2 * j + half
                    ctx.do("vector", "scalar_tensor_tensor",
                           xt(S, tc)[:, half * 512:(half + 1) * 512],
                           bank(b),
                           S["c1al"][:, tc * E + e:tc * E + e + 1],
                           xt(S, tc)[:, half * 512:(half + 1) * 512],
                           ALU.mult, ALU.add, inc="dv" if half == 1 else None)
                bank_free[2 * j] = ctx.tok("dv")
                bank_free[2 * j + 1] = ctx.tok("dv")
                x_marks.append(ctx.tok("dv"))
    st["x_writes"] = [x_marks[-1]]
    st["xn_consumers"].append(ctx.tok("pe"))
    st["dyn_pe_free"] = ctx.tok("pe")
    st["dyn_dv_free"] = ctx.tok("dv")
    dump_buf(ctx, st, "x_m1", S["x"][:], x_marks[-1])


PHASE_EMITTERS = {
    "ln0": lambda c, s: emit_ln(c, s, 0),
    "ln1": lambda c, s: emit_ln(c, s, 1),
    "attn0": emit_attn0,
    "rt0": emit_router0,
    "moe0": emit_moe0,
    "rt1": emit_router1,
    "attn1": emit_attn1,
    "moe1": emit_moe1,
}



PLAN = [["ln0", "rt0", "attn0", "moe0", "ln1", "rt1", "attn1", "moe1"]]


def kernel(**inputs):
    per_core = host_prep(inputs)
    for launch in PLAN:
        nc, es, in_names = build_program(launch)
        in_maps = [{k: per_core[c][k] for k in in_names} for c in range(N_CORES)]
        res = run_bass_kernel_spmd(nc, in_maps, core_ids=list(range(N_CORES)))
        es.close()
        for c in range(N_CORES):
            per_core[c]["x_in"] = np.asarray(res.results[c]["x_out"])
    out = np.concatenate([per_core[c]["x_in"] for c in range(N_CORES)], axis=0)
    return out.reshape(2, 8192, D).astype(np.float32)


# version 03
# speedup vs baseline: 1.0216x; 1.0216x over previous
import numpy as np
import ml_dtypes
import concourse.bass as bass
import concourse.mybir as mybir
from concourse.bass_utils import run_bass_kernel_spmd
from concourse.replica_groups import maybe_share_collective_output_space
from contextlib import ExitStack

F32 = mybir.dt.float32
BF16 = mybir.dt.bfloat16
U32 = mybir.dt.uint32
AF = mybir.ActivationFunctionType
ALU = mybir.AluOpType
AX = mybir.AxisListType
BF = ml_dtypes.bfloat16

N_CORES = 8
T = 2048
NT = 16
D = 1024
CC = 8
F0 = 4096
NF0 = 32
H1 = 2048
NH1 = 16
E = 8
CAP = 2560.0

GROUPS_SEQ = [[0, 1, 2, 3], [4, 5, 6, 7]]
GROUPS_ALL = [[0, 1, 2, 3, 4, 5, 6, 7]]

ENGINES = ("sync", "tensor", "vector", "scalar", "gpsimd")

DYN_N = 38912
ATTN0_NT2 = NT
ATTN0_SKIP = ()
MOE0_NE = E
MOE0_NTH = 2
MOE0_NGRP = 2
MOE0_HEAVY = False
MOE0_BAR = 0
MOE0_ESTART = 0
ATTN0_UPTO = 99


def _t(a):
    return np.ascontiguousarray(a)



def host_prep(inp):
    cm = {}
    x = np.asarray(inp["L0_attn_x"], dtype=np.float32)
    xf = x.reshape(16384, D)

    wqkv = np.asarray(inp["L0_attn_W_qkv"], dtype=np.float32)
    cm["wqkvT"] = _t(wqkv.T.reshape(CC, 128, 3072))
    wproj = np.asarray(inp["L0_attn_W_proj"], dtype=np.float32)
    cm["wprojT"] = _t(wproj.T.reshape(CC, 128, D))
    cm["bproj"] = _t(np.asarray(inp["L0_attn_b_proj"], np.float32).reshape(1, D))

    wr = np.asarray(inp["L0_ffn_Wr"], dtype=np.float32)
    cm["wr0"] = _t(wr.reshape(CC, 128, E))
    cm["br0"] = _t(np.asarray(inp["L0_ffn_br"], np.float32).reshape(1, E))
    for nm, key in (("wi0", "L0_ffn_Wi"), ("wg0", "L0_ffn_Wg")):
        w = np.asarray(inp[key], dtype=np.float32)
        cm[nm] = _t(w.reshape(E, CC, 128, NF0, 128).transpose(0, 3, 2, 1, 4))
    cm["wo0"] = _t(np.asarray(inp["L0_ffn_Wo"], np.float32)
                   .reshape(E, NF0, 128, D))
    for nm, key in (("bi0", "L0_ffn_bi"), ("bg0", "L0_ffn_bg")):
        b = np.asarray(inp[key], dtype=np.float32)
        cm[nm] = _t(b.reshape(E, NF0, 128).transpose(2, 0, 1).reshape(128, E * NF0))
    cm["bo0"] = _t(np.asarray(inp["L0_ffn_bo"], np.float32))
    g = np.asarray(inp["L0_ffn_gamma"], dtype=np.float32)
    b = np.asarray(inp["L0_ffn_beta"], dtype=np.float32)
    cm["gam0"] = _t(np.broadcast_to(g[:, None, :], (E, 128, D)))
    cm["bet0"] = _t(np.broadcast_to(b[:, None, :], (E, 128, D)))

    for nm, key in (("wq1", "L1_attn_Wq"), ("wk1", "L1_attn_Wk")):
        w = np.asarray(inp[key], dtype=np.float32)
        wt = w.T.reshape(CC, 128, CC, 128)
        cm[nm] = _t(wt.transpose(2, 1, 0, 3))
    for nm, key in (("wv1", "L1_attn_Wv"), ("wo1", "L1_attn_Wo")):
        w = np.asarray(inp[key], dtype=np.float32)
        cm[nm] = _t(w.T.reshape(CC, 128, D))
    cm["bv1row"] = _t(np.asarray(inp["L1_attn_bv"], np.float32).reshape(1, D))
    for nm, key in (("bq1", "L1_attn_bq"), ("bk1", "L1_attn_bk"),
                    ("bv1", "L1_attn_bv")):
        cm[nm] = _t(np.asarray(inp[key], np.float32).reshape(CC, 128).T)
    cm["bo1"] = _t(np.asarray(inp["L1_attn_bo"], np.float32).reshape(1, D))

    gw = np.asarray(inp["L1_ffn_gate_w"], dtype=np.float32)
    cm["gw1"] = _t(gw.T.reshape(CC, 128, E))
    for nm, key in (("wg1", "L1_ffn_gate_proj_w"), ("wu1", "L1_ffn_up_proj_w")):
        w = np.asarray(inp[key], dtype=np.float32)
        wt = w.transpose(0, 2, 1).reshape(E, CC, 128, NH1, 128)
        cm[nm] = _t(wt.transpose(0, 3, 2, 1, 4))
    wd = np.asarray(inp["L1_ffn_down_proj_w"], dtype=np.float32)
    cm["wd1"] = _t(wd.transpose(0, 2, 1).reshape(E, NH1, 128, D))

    cm["idn"] = np.eye(128, dtype=BF)
    tri = np.tril(np.ones((128, 128), dtype=np.float32))
    cm["mask_qk"] = tri.astype(BF)
    cm["mask_kq"] = _t(tri.T).astype(BF)
    cm["lincl"] = _t(np.triu(np.ones((128, 128), dtype=np.float32)))
    cm["ones_col"] = np.ones((128, 1), dtype=BF)
    cm["ones1"] = np.ones((1, 128), dtype=BF)
    cm["ones1f"] = np.ones((1, 128), dtype=np.float32)
    cm["onesf"] = np.ones((128, 1), dtype=np.float32)
    cm["iota8"] = _t(np.broadcast_to(np.arange(8, dtype=np.float32)[None, :],
                                     (128, 8)))
    cm["epsc"] = _t(np.broadcast_to(
        np.array([1e-5, 1e-12], dtype=np.float32)[None, :], (128, 2)))
    cm["idnf"] = np.eye(128, dtype=np.float32)

    per_core = []
    for c in range(N_CORES):
        d = dict(cm)
        d["x_in"] = _t(xf[c * T:(c + 1) * T])
        grp = GROUPS_SEQ[0] if c < 4 else GROUPS_SEQ[1]
        pm = np.array([1.0 if (r in grp and r < c) else 0.0 for r in range(8)],
                      dtype=np.float32)
        d["pref_mask"] = _t(np.broadcast_to(pm[None, :], (128, 8)))
        om = (np.arange(8) < c).astype(np.float32)
        d["off_mask"] = _t(np.broadcast_to(om[:, None], (8, 128)))
        per_core.append(d)
    return per_core



class Ctx:
    ESEM = {"tensor": "pe", "vector": "dv", "scalar": "ac", "gpsimd": "po"}

    def __init__(self, nc, es):
        self.nc = nc
        self.es = es
        self.q = {e: [] for e in ENGINES}
        self.sems = {}
        self.cnt = {}
        for name in ("pe", "dv", "ac", "po", "sy", "gp", "st",
                     "c0", "c1", "c2"):
            self.sems[name] = es.enter_context(nc.semaphore(name))
            self.cnt[name] = 0
        self.sb = {}
        self.dr = {}
        self.ps = {}

    def do(self, eng, fname, *a, inc=None, incn=1, waits=(), **k):
        sems = self.sems
        waits = tuple(w for w in waits if w is not None)

        def closure(e, fname=fname, a=a, k=k, waits=waits, inc=inc, incn=incn):
            for s, v in waits:
                e.wait_ge(sems[s], v)
            r = getattr(e, fname)(*a, **k)
            if inc is not None:
                r.then_inc(sems[inc], incn)
        self.q[eng].append(closure)
        if inc is not None:
            self.cnt[inc] += incn
            return (inc, self.cnt[inc])
        return None

    def wait(self, eng, *marks):
        marks = tuple(m for m in marks if m is not None)
        if not marks:
            return
        sems = self.sems

        def closure(e, marks=marks):
            for s, v in marks:
                e.wait_ge(sems[s], v)
        self.q[eng].append(closure)

    def tok(self, sem):
        return (sem, self.cnt[sem])

    def barrier(self, engines=("tensor", "vector", "scalar", "gpsimd")):
        marks = []
        for eng in engines:
            sem = self.ESEM[eng]
            self.do(eng, "nop", inc=sem)
            marks.append(self.tok(sem))
        for eng in engines:
            self.wait(eng, *marks)

    def finalize(self):
        with self.nc.Block() as block:
            for eng in ENGINES:
                ops = self.q[eng]
                if not ops:
                    continue

                def body(e, ops=ops):
                    for f in ops:
                        f(e)
                getattr(block, eng)(body)


def later(a, b):
    if a is None:
        return b
    if b is None:
        return a
    assert a[0] == b[0]
    return a if a[1] >= b[1] else b



DUMP_SHAPES = {
    "xn0": ([128, CC * T], BF16), "xn1": ([128, CC * T], BF16),
    "x_a0": ([128, NT * D], F32), "x_m0": ([128, NT * D], F32),
    "x_a1": ([128, NT * D], F32), "x_m1": ([128, NT * D], F32),
    "vk_dbg": ([128, 264], F32),
    "c0_dbg": ([128, NT * E], F32), "c1_dbg": ([128, NT * E], F32),
    "st_dbg": ([128, 520], F32),
    "h_dbg": ([128, 32768], BF16),
    "r_dbg": ([128, 1024], F32),
}


def build_program(phases, dump=()):
    nc = bass.Bass("TRN2", target_bir_lowering=False)
    es = ExitStack()
    ctx = Ctx(nc, es)
    ctx.dump = set(dump)
    need = set(phases)

    in_names = []

    def din(name, shape, dtype=F32):
        in_names.append(name)
        ctx.dr[name] = nc.dram_tensor(name, shape, dtype, kind="ExternalInput").ap()

    def dout(name, shape, dtype=F32):
        ctx.dr[name] = nc.dram_tensor(name, shape, dtype, kind="ExternalOutput").ap()

    din("x_in", [T, D])
    dout("x_out", [T, D])
    for nm in ctx.dump:
        dout(nm, *DUMP_SHAPES[nm])
    for cn, sh, dt in (("idn", [128, 128], BF16), ("mask_qk", [128, 128], BF16),
                       ("mask_kq", [128, 128], BF16), ("lincl", [128, 128], F32),
                       ("ones_col", [128, 1], BF16), ("ones1", [1, 128], BF16),
                       ("ones1f", [1, 128], F32),
                       ("onesf", [128, 1], F32), ("iota8", [128, 8], F32),
                       ("epsc", [128, 2], F32), ("idnf", [128, 128], F32),
                       ("pref_mask", [128, 8], F32), ("off_mask", [8, 128], F32)):
        din(cn, sh, dt)
    if "attn0" in need:
        din("wqkvT", [CC, 128, 3072])
        din("wprojT", [CC, 128, D])
        din("bproj", [1, D])
        ctx.dr["vk_in"] = nc.dram_tensor("vk_in", [128, 264], F32).ap()
        sp = maybe_share_collective_output_space("AllReduce", GROUPS_SEQ)
        ctx.dr["vk_out"] = nc.dram_tensor("vk_out", [128, 264], F32,
                                          addr_space=sp).ap()
    if "rt0" in need or "moe0" in need:
        din("wr0", [CC, 128, E])
        din("br0", [1, E])
    if "moe0" in need:
        din("wi0", [E, NF0, 128, CC, 128])
        din("wg0", [E, NF0, 128, CC, 128])
        din("wo0", [E, NF0, 128, D])
        din("bi0", [128, E * NF0])
        din("bg0", [128, E * NF0])
        din("bo0", [E, D])
        din("gam0", [E, 128, D])
        din("bet0", [E, 128, D])
    if "attn1" in need:
        for w in ("wq1", "wk1"):
            din(w, [CC, 128, CC, 128])
        for w in ("wv1", "wo1"):
            din(w, [CC, 128, D])
        for bnm in ("bq1", "bk1"):
            din(bnm, [128, CC])
        din("bv1row", [1, D])
        din("bo1", [1, D])
        ctx.dr["s_in"] = nc.dram_tensor("s_in", [128, 520], F32).ap()
        sp = maybe_share_collective_output_space("AllGather", GROUPS_ALL)
        ctx.dr["s_out"] = nc.dram_tensor("s_out", [8 * 128, 520], F32,
                                         addr_space=sp).ap()
    if "rt1" in need or "moe1" in need:
        din("gw1", [CC, 128, E])
        ctx.dr["r_in"] = nc.dram_tensor("r_in", [1, 16], F32).ap()
        sp = maybe_share_collective_output_space("AllGather", GROUPS_ALL)
        ctx.dr["r_out"] = nc.dram_tensor("r_out", [8, 16], F32,
                                         addr_space=sp).ap()
    if "moe1" in need:
        din("wg1", [E, NH1, 128, CC, 128])
        din("wu1", [E, NH1, 128, CC, 128])
        din("wd1", [E, NH1, 128, D])

    def sb(name, shape, dtype):
        h = es.enter_context(nc.sbuf_tensor(name, shape, dtype))
        ctx.sb[name] = h
        return h

    sb("x", [128, NT * D], F32)
    sb("xn", [128, CC * T], BF16)
    sb("lnst", [128, 64], F32)
    sb("dyn", [128, DYN_N], BF16)
    sb("rt", [128, D], F32)
    sb("ftmp", [128, 1024], F32)
    sb("tmp", [128, 2048], BF16)
    sb("gamb", [128, D], F32)
    sb("betb", [128, D], F32)
    sb("borow", [4, D], BF16)
    sb("cst", [128, 128], BF16)
    sb("mqk", [128, 128], BF16)
    sb("mkq", [128, 128], BF16)
    sb("linc", [128, 128], F32)
    sb("onec", [128, 1], BF16)
    sb("one1", [1, 128], BF16)
    sb("one1f", [1, 128], F32)
    sb("onef", [128, 1], F32)
    sb("io8", [128, 8], F32)
    sb("pmask", [128, 8], F32)
    sb("omask", [8, 128], F32)
    sb("bsm", [128, 32], F32)
    sb("bib", [128, E * NF0], F32)
    sb("bgh", [128, E * NF0], F32)
    sb("call", [128, NT * E], F32)
    sb("c1al", [128, NT * E], F32)
    sb("m12", [128, NT * 16], F32)
    sb("wn12", [128, NT * 2], F32)
    sb("rowt", [8, 16], F32)
    sb("carry", [1, 16], F32)
    sb("cnt16", [128, 16], F32)
    sb("sml", [128, 32], F32)
    sb("epss", [128, 2], F32)
    sb("wrsb", [128, CC * E], BF16)
    sb("brrow", [1, 16], BF16)
    sb("bvrow", [1, D], BF16)
    sb("idnfs", [128, 128], F32)
    sb("wrf", [128, 64], F32)
    sb("brf", [1, 16], F32)

    dynh = ctx.sb["dyn"]
    ctx.sb["vk"] = dynh[:, 37120:37648].bitcast(F32)
    ctx.sb["vkbf"] = dynh[:, 37648:37912]
    ctx.sb["Ssb"] = dynh[:, 37120:38160].bitcast(F32)
    ctx.sb["Sbf"] = dynh[:, 38160:38680]
    ctx.sb["gath"] = dynh[:, 0:1040].bitcast(F32)

    pb = []
    for i in range(8):
        if i >= 6:
            h = es.enter_context(nc.psum_tensor("pt%d" % i, [128, 1024], BF16))
        else:
            h = es.enter_context(nc.psum_tensor("pb%d" % i, [128, 512], F32))
        pb.append(h)
    ctx.pb = pb

    def bank(i):
        if i >= 6:
            return pb[i][:].bitcast(F32)
        return pb[i][:]

    def bankb(i):
        return pb[i][:]

    ctx.bank = bank
    ctx.bankb = bankb

    S = ctx.sb

    x3 = ctx.dr["x_in"].rearrange("(a p) d -> p a d", p=128)
    xs3 = S["x"][:].rearrange("p (a d) -> p a d", d=D)
    for half in range(2):
        ctx.do("sync", "dma_start",
               out=xs3[:, half * 8:(half + 1) * 8, :],
               in_=x3[:, half * 8:(half + 1) * 8, :],
               inc="sy", incn=16)
    for dst, src in (("cst", "idn"), ("mqk", "mask_qk"), ("mkq", "mask_kq"),
                     ("linc", "lincl"), ("onec", "ones_col"), ("one1", "ones1"),
                     ("one1f", "ones1f"),
                     ("onef", "onesf"), ("io8", "iota8"), ("pmask", "pref_mask"),
                     ("epss", "epsc"), ("idnfs", "idnf"),
                     ("omask", "off_mask")):
        ctx.do("sync", "dma_start", out=S[dst][:], in_=ctx.dr[src],
               inc="sy", incn=16)
    boot = ctx.tok("sy")

    st = dict(boot=boot, x_ready=boot, x_writes=[], xn_consumers=[])

    for ph in phases:
        PHASE_EMITTERS[ph](ctx, st)

    ctx.wait("sync", *st["x_writes"])
    xo3 = ctx.dr["x_out"].rearrange("(a p) d -> p a d", p=128)
    for half in range(2):
        ctx.do("sync", "dma_start",
               out=xo3[:, half * 8:(half + 1) * 8, :],
               in_=xs3[:, half * 8:(half + 1) * 8, :],
               inc="st", incn=16)
    ctx.wait("sync", ctx.tok("st"))

    ctx.finalize()
    return nc, es, in_names


def xt(S, tc):
    return S["x"][:, tc * D:(tc + 1) * D]


def xnf(S, cc, lo, n):
    return S["xn"][:, cc * T + lo:cc * T + lo + n]


def dmaw(ctx, dst_ap, src_ap):
    return ctx.do("gpsimd", "dma_start", out=dst_ap, in_=src_ap,
                  inc="gp", incn=16)


def dump_buf(ctx, st, name, ap, waitmark):
    if name in ctx.dump:
        ctx.wait("sync", waitmark)
        ctx.do("sync", "dma_start", out=ctx.dr[name], in_=ap, inc="st", incn=16)



def emit_ln(ctx, st, layer):
    S = ctx.sb
    ctx.wait("vector", *st["x_writes"], st["x_ready"])
    for tc in range(NT):
        ctx.do("vector", "bn_stats", S["lnst"][:, 48:54], xt(S, tc)[:, 0:512])
        ctx.do("vector", "bn_stats", S["lnst"][:, 54:60], xt(S, tc)[:, 512:1024])
        ctx.do("vector", "drain")
        ctx.do("vector", "bn_aggr", S["lnst"][:, 60:62], S["lnst"][:, 48:60])
        ctx.do("vector", "drain")
        ctx.do("vector", "tensor_copy", S["lnst"][:, tc:tc + 1],
               S["lnst"][:, 60:61])
        ctx.do("vector", "tensor_copy", S["lnst"][:, 16 + tc:17 + tc],
               S["lnst"][:, 61:62], inc="dv")
    mvar = ctx.tok("dv")
    ctx.wait("scalar", mvar)
    ctx.do("scalar", "activation", S["lnst"][:, 32:48], S["lnst"][:, 16:32],
           AF.Sqrt, bias=S["epss"][:, 0:1], scale=1.0, inc="ac")
    ctx.wait("vector", ctx.tok("ac"))
    ctx.do("vector", "reciprocal", S["lnst"][:, 16:32], S["lnst"][:, 32:48])
    ctx.do("vector", "drain")
    ctx.do("vector", "tensor_mul", S["lnst"][:, 32:48], S["lnst"][:, 0:16],
           S["lnst"][:, 16:32])
    ctx.do("vector", "drain")
    ctx.do("vector", "tensor_scalar", S["lnst"][:, 32:48], S["lnst"][:, 32:48],
           -1.0, None, ALU.mult, inc="dv")
    st["stats_ready"] = ctx.tok("dv")

    ctx.wait("vector", *st["xn_consumers"])
    pe_done = [None, None]
    dv_done = [None, None]
    for tc in range(NT):
        par = tc % 2
        buf = S["tmp"][:, par * 1024:(par + 1) * 1024]
        ctx.wait("vector", pe_done[par])
        ctx.do("vector", "tensor_scalar", buf, xt(S, tc),
               S["lnst"][:, tc:tc + 1], S["lnst"][:, 16 + tc:17 + tc],
               ALU.subtract, ALU.mult, inc="dv")
        m = ctx.tok("dv")
        pt = ctx.bankb(6 + par)
        ctx.wait("tensor", m, dv_done[par])
        for j in range(CC):
            ctx.do("tensor", "transpose", pt[:, j * 128:(j + 1) * 128],
                   buf[:, j * 128:(j + 1) * 128], S["cst"][:],
                   inc="pe" if j == CC - 1 else None)
        pe_done[par] = ctx.tok("pe")
        ctx.wait("vector", pe_done[par])
        for j in range(CC):
            ctx.do("vector", "tensor_copy",
                   xnf(S, j, tc * 128, 128), pt[:, j * 128:(j + 1) * 128],
                   inc="dv" if j == CC - 1 else None)
        dv_done[par] = ctx.tok("dv")
    st["xn_ready"] = ctx.tok("dv")
    st["xn_consumers"] = []
    dump_buf(ctx, st, "xn%d" % layer, S["xn"][:], st["xn_ready"])



def emit_attn0(ctx, st):
    S = ctx.sb
    nc = ctx.nc
    bank, bankb = ctx.bank, ctx.bankb
    QFM = 0
    WKV = 16384
    VST = 32768
    KST = 35072
    AST = 37120
    dyn = S["dyn"]

    def dslice(off, n):
        return dyn[:, off:off + n]

    wkv_m = dmaw(ctx, dslice(WKV, 16384).rearrange("p (c m) -> p c m", c=CC),
                 ctx.dr["wqkvT"][:, :, 1024:3072]
                 .rearrange("c p m -> p c m"))
    xn_rdy = st["xn_ready"]

    ctx.do("vector", "memset", S["vk"][:], 0.0)
    for par in range(2):
        v_ap = dslice(VST + par * 1088, 1056)
        ctx.do("vector", "memset",
               v_ap.rearrange("p (h t) -> p h t", t=33)[:, :, 32:33], 1.0,
               inc="dv" if par == 1 else None)
    vst_init = ctx.tok("dv")

    ctx.wait("tensor", wkv_m, xn_rdy, vst_init)
    kv_banks = [0, 1, 2, 3]
    vk_banks = [4, 5]
    ev_done = {}
    vk_done = {}
    for tc in range(NT):
        par = tc % 2
        kst = dslice(KST + par * 1024, 1024)
        vst = dslice(VST + par * 1088, 1056)
        seg_marks = []
        for seg in range(4):
            b = kv_banks[seg]
            ctx.wait("tensor", ev_done.get(b))
            for cc in range(CC):
                ctx.do("tensor", "matmul", bank(b),
                       xnf(S, cc, tc * 128, 128),
                       dslice(WKV + cc * 2048 + seg * 512, 512),
                       start=(cc == 0), stop=(cc == CC - 1),
                       inc="pe" if cc == CC - 1 else None)
            seg_marks.append(ctx.tok("pe"))
        for seg in range(2):
            b = kv_banks[seg]
            ctx.wait("scalar", seg_marks[seg], ev_done.get(("kwr", par)))
            ctx.do("scalar", "activation", kst[:, seg * 512:(seg + 1) * 512],
                   bank(b), AF.Relu, inc="ac")
            ev_done[b] = ctx.tok("ac")
        for seg in range(2, 4):
            b = kv_banks[seg]
            ctx.wait("vector", seg_marks[seg], ev_done.get(("vwr", par)))
            src = bank(b).rearrange("p (h t) -> p h t", t=32)
            dst = vst.rearrange("p (h t) -> p h t", t=33)[
                :, seg * 16 - 32:seg * 16 - 16, 0:32]
            ctx.do("vector", "tensor_copy", dst, src,
                   inc="dv" if seg == 3 else None)
        kv_rdy_k = ev_done[kv_banks[1]]
        kv_rdy_v = ctx.tok("dv")
        ev_done[kv_banks[2]] = kv_rdy_v
        ev_done[kv_banks[3]] = kv_rdy_v
        del kv_rdy_k
        for hg in range(4):
            b = vk_banks[hg % 2]
            pbase = hg * 32
            ctx.wait("tensor", ev_done[kv_banks[1]], kv_rdy_v, vk_done.get(b))
            for i in range(8):
                h = hg * 8 + i
                ctx.do("tensor", "matmul",
                       bank(b)[pbase:pbase + 32, i * 33:i * 33 + 33],
                       kst[:, h * 32:h * 32 + 32],
                       vst[:, h * 33:h * 33 + 33],
                       start=True, stop=True, tile_position=(0, pbase),
                       inc="pe" if i == 7 else None)
            mvk = ctx.tok("pe")
            ctx.wait("vector", mvk)
            ctx.do("vector", "tensor_add",
                   S["vk"][pbase:pbase + 32, 0:264],
                   S["vk"][pbase:pbase + 32, 0:264],
                   bank(b)[pbase:pbase + 32, 0:264], inc="dv")
            vk_done[b] = ctx.tok("dv")
        ev_done[("kwr", par)] = ctx.tok("pe")
        ev_done[("vwr", par)] = ctx.tok("pe")
    vk_local = ctx.tok("dv")
    dump_buf(ctx, st, "vk_dbg", S["vk"][:], vk_local)
    if ATTN0_UPTO < 2:
        st["xn_consumers"].append(ctx.tok("pe"))
        return

    pass1_pe = ctx.tok("pe")
    ctx.wait("gpsimd", pass1_pe)
    wq_m = dmaw(ctx, dslice(WKV, 8192).rearrange("p (c m) -> p c m", c=CC),
                ctx.dr["wqkvT"][:, :, 0:1024].rearrange("c p m -> p c m"))

    ctx.wait("sync", vk_local)
    ctx.do("sync", "dma_start", out=ctx.dr["vk_in"], in_=S["vk"][:],
           inc="sy", incn=16)
    ctx.wait("gpsimd", ctx.tok("sy"))
    ctx.do("gpsimd", "collective_compute", "AllReduce", ALU.add,
           replica_groups=GROUPS_SEQ,
           ins=[ctx.dr["vk_in"]], outs=[ctx.dr["vk_out"]], inc="c0")
    ctx.wait("gpsimd", ctx.tok("c0"))
    vo = ctx.dr["vk_out"].rearrange("(g e) (s d) -> g e s d", g=4, d=33)
    for b4 in range(4):
        for g in range(4):
            ap = vo[g, :, b4:b4 + 5:4, :]
            ctx.do("gpsimd", "dma_start",
                   out=S["vkbf"][b4 * 32:(b4 + 1) * 32,
                                 g * 66:(g + 1) * 66]
                   .rearrange("e (s d) -> e s d", d=33),
                   in_=ap, inc="gp", incn=16)
    vkbf_rdy = ctx.tok("gp")
    if ATTN0_UPTO < 3:
        ctx.wait("sync", vkbf_rdy)
        st["xn_consumers"].append(ctx.tok("pe"))
        return

    q_banks = [0, 1]
    trp = [6, 7]
    ev_q = {}
    pe_tr = {}
    dv_tr = {}
    ctx.wait("tensor", wq_m, pass1_pe)
    for tc in range(NT):
        par = tc % 2
        qst = dslice(KST + par * 1024, 1024)
        segm = []
        for seg in range(2):
            b = q_banks[seg]
            ctx.wait("tensor", ev_q.get(b))
            for cc in range(CC):
                ctx.do("tensor", "matmul", bank(b),
                       xnf(S, cc, tc * 128, 128),
                       dslice(WKV + cc * 1024 + seg * 512, 512),
                       start=(cc == 0), stop=(cc == CC - 1),
                       inc="pe" if cc == CC - 1 else None)
            segm.append(ctx.tok("pe"))
        for seg in range(2):
            b = q_banks[seg]
            ctx.wait("scalar", segm[seg], pe_tr.get(par))
            ctx.do("scalar", "activation", qst[:, seg * 512:(seg + 1) * 512],
                   bank(b), AF.Relu, inc="ac")
            ev_q[b] = ctx.tok("ac")
        mq = ctx.tok("ac")
        pt = bankb(trp[par])
        ctx.wait("tensor", mq, dv_tr.get(par))
        for j in range(CC):
            ctx.do("tensor", "transpose", pt[:, j * 128:(j + 1) * 128],
                   qst[:, j * 128:(j + 1) * 128], S["cst"][:],
                   inc="pe" if j == CC - 1 else None)
        pe_tr[par] = ctx.tok("pe")
        ctx.wait("vector", pe_tr[par])
        for j in range(CC):
            ctx.do("vector", "tensor_copy",
                   dslice(QFM + j * 2048 + tc * 128, 128),
                   pt[:, j * 128:(j + 1) * 128],
                   inc="dv" if j == CC - 1 else None)
        dv_tr[par] = ctx.tok("dv")
    q_rdy = ctx.tok("dv")
    if ATTN0_UPTO < 4:
        st["xn_consumers"].append(ctx.tok("pe"))
        return

    ctx.wait("gpsimd", pass1_pe)
    wp_m = dmaw(ctx, dslice(WKV + 8192, 8192).rearrange("p (c m) -> p c m", c=CC),
                ctx.dr["wprojT"].rearrange("c p m -> p c m"))
    bp_m = dmaw(ctx, S["borow"][0:1, :], ctx.dr["bproj"])
    ob_done = {}
    pr_done = None
    x_marks = []
    ctx.wait("tensor", q_rdy, vkbf_rdy)
    for tc in range(ATTN0_NT2):
        par = tc % 2
        atm = dslice(VST + par * 1088, 1056)[:, 0:1024]
        afm = dslice(KST + par * 1024, 1024)
        ctx.wait("tensor", *[ob_done.get(2 + j) for j in range(4)])
        for h in range(32):
            base = (h % 4) * 32
            b = 2 + (h % 4)
            s_ = h // 4
            vkc = (h // 8) * 2 + (h % 8) // 4
            ctx.do("tensor", "matmul",
                   bank(b)[:, s_ * 33:s_ * 33 + 33],
                   dyn[base:base + 32,
                       QFM + (h // 4) * 2048 + tc * 128:
                       QFM + (h // 4) * 2048 + tc * 128 + 128],
                   S["vkbf"][base:base + 32, vkc * 33:vkc * 33 + 33],
                   start=True, stop=True, tile_position=(base, 0),
                   inc="pe" if h == 31 else None)
        mg = ctx.tok("pe")
        ctx.wait("vector", mg)
        for j in range(4):
            b = 2 + j
            pv = bank(b)[:, 0:264].rearrange("p (h t) -> p h t", t=33)
            ctx.do("vector", "tensor_scalar", S["sml"][:, 0:8],
                   pv[:, :, 32], 1e-15, None, ALU.add)
            ctx.do("vector", "drain")
            ctx.do("vector", "reciprocal", S["sml"][:, 8:16], S["sml"][:, 0:8])
            ctx.do("vector", "drain")
            for s_ in range(8):
                h = s_ * 4 + j
                ctx.do("vector", "tensor_scalar",
                       atm[:, h * 32:h * 32 + 32],
                       bank(b)[:, s_ * 33:s_ * 33 + 32],
                       S["sml"][:, 8 + s_:9 + s_], None, ALU.mult,
                       inc="dv" if s_ == 7 else None)
            ob_done[b] = ctx.tok("dv")
        m_at = ctx.tok("dv")
        if "tr" in ATTN0_SKIP:
            continue
        pt = bankb(trp[par])
        ctx.wait("tensor", m_at, ob_done.get("dvaf%d" % par))
        for j in range(CC):
            ctx.do("tensor", "transpose", pt[:, j * 128:(j + 1) * 128],
                   atm[:, j * 128:(j + 1) * 128], S["cst"][:],
                   inc="pe" if j == CC - 1 else None)
        mtr = ctx.tok("pe")
        ctx.wait("vector", mtr)
        for j in range(CC):
            ctx.do("vector", "tensor_copy", afm[:, j * 128:(j + 1) * 128],
                   pt[:, j * 128:(j + 1) * 128],
                   inc="dv" if j == CC - 1 else None)
        ob_done["dvaf%d" % par] = ctx.tok("dv")
        maf = ctx.tok("dv")
        if "proj" in ATTN0_SKIP:
            continue
        b0, b1 = 0, 1
        ctx.wait("tensor", maf, wp_m, bp_m, pr_done)
        for half in range(2):
            bb = (b0, b1)[half]
            for j in range(CC):
                ctx.do("tensor", "matmul", bank(bb),
                       afm[:, j * 128:(j + 1) * 128],
                       dslice(WKV + 8192 + j * 1024 + half * 512, 512),
                       start=(j == 0), stop=False)
            ctx.do("tensor", "matmul", bank(bb), S["one1"][:],
                   S["borow"][0:1, half * 512:(half + 1) * 512],
                   start=False, stop=True,
                   inc="pe")
        mpr = ctx.tok("pe")
        ctx.wait("vector", mpr)
        ctx.do("vector", "tensor_add", xt(S, tc)[:, 0:512], xt(S, tc)[:, 0:512],
               bank(b0))
        ctx.do("vector", "tensor_add", xt(S, tc)[:, 512:1024],
               xt(S, tc)[:, 512:1024], bank(b1), inc="dv")
        pr_done = ctx.tok("dv")
        x_marks.append(pr_done)
    if x_marks:
        st["x_writes"] = [x_marks[-1]]
        dump_buf(ctx, st, "x_a0", S["x"][:], x_marks[-1])
    st["xn_consumers"].append(ctx.tok("pe"))
    st["bo_pe_free"] = ctx.tok("pe")
    st["dyn_pe_free"] = ctx.tok("pe")
    st["dyn_dv_free"] = ctx.tok("dv")





def emit_router0(ctx, st):
    S = ctx.sb
    bank = ctx.bank
    wr_m = ctx.do("sync", "dma_start",
                  out=S["wrf"][:].rearrange("p (c e) -> p c e", e=E),
                  in_=ctx.dr["wr0"].rearrange("c p e -> p c e"),
                  inc="sy", incn=16)
    br_m = ctx.do("sync", "dma_start", out=S["brf"][0:1, 0:E],
                  in_=ctx.dr["br0"], inc="sy", incn=16)
    ctx.wait("tensor", wr_m, br_m, st["stats_ready"])
    lg_done = None
    tr_done = None
    ev_done = None
    for tc in range(NT):
        b = 4 + tc % 2
        ctx.wait("vector", tr_done)
        ctx.do("vector", "tensor_scalar", S["ftmp"][:], xt(S, tc),
               S["lnst"][:, tc:tc + 1], S["lnst"][:, 16 + tc:17 + tc],
               ALU.subtract, ALU.mult, inc="dv")
        mx = ctx.tok("dv")
        ctx.wait("tensor", mx, ev_done, tr_done)
        for j in range(CC):
            ctx.do("tensor", "transpose",
                   bank(j // 4)[:, (j % 4) * 128:(j % 4 + 1) * 128],
                   S["ftmp"][:, j * 128:(j + 1) * 128], S["idnfs"][:],
                   inc="pe" if j == CC - 1 else None)
        mtr = ctx.tok("pe")
        ctx.wait("vector", mtr)
        ctx.do("vector", "tensor_copy", S["rt"][:, 0:512], bank(0))
        ctx.do("vector", "tensor_copy", S["rt"][:, 512:1024], bank(1),
               inc="dv")
        tr_done = ctx.tok("dv")
        ctx.wait("tensor", tr_done, lg_done)
        for cc in range(CC):
            ctx.do("tensor", "matmul", bank(b)[:, 0:E],
                   S["rt"][:, cc * 128:(cc + 1) * 128],
                   S["wrf"][:, cc * E:(cc + 1) * E],
                   start=(cc == 0), stop=False)
        ctx.do("tensor", "matmul", bank(b)[:, 0:E], S["one1f"][:],
               S["brf"][0:1, 0:E], start=False, stop=True, inc="pe")
        ev_done = ctx.tok("pe")
        ml = ctx.tok("pe")
        ctx.wait("vector", ml)
        ctx.do("vector", "reduce_max", S["sml"][:, 0:1], bank(b)[:, 0:E],
               AX.X, negate=True, inc="dv")
        ctx.wait("scalar", ctx.tok("dv"))
        ctx.do("scalar", "activation", S["ftmp"][:, 0:E], bank(b)[:, 0:E],
               AF.Exp, bias=S["sml"][:, 0:1], scale=1.0, inc="ac")
        lg_done = ctx.tok("ac")
        ctx.wait("vector", lg_done)
        ctx.do("vector", "max", S["sml"][:, 16:24], S["ftmp"][:, 0:E])
        ctx.do("vector", "drain")
        ctx.do("vector", "max_index", S["sml"][:, 24:32].bitcast(U32),
               S["sml"][:, 16:24], S["ftmp"][:, 0:E])
        ctx.do("vector", "drain")
        ctx.do("vector", "tensor_add", S["sml"][:, 1:2], S["sml"][:, 16:17],
               S["sml"][:, 17:18])
        ctx.do("vector", "tensor_copy", S["sml"][:, 4:6],
               S["sml"][:, 24:32].bitcast(U32)[:, 0:2])
        ctx.do("vector", "drain")
        ctx.do("vector", "reciprocal", S["sml"][:, 2:3], S["sml"][:, 1:2])
        ctx.do("vector", "drain")
        ctx.do("vector", "tensor_mul", S["sml"][:, 6:7], S["sml"][:, 16:17],
               S["sml"][:, 2:3])
        ctx.do("vector", "tensor_mul", S["sml"][:, 7:8], S["sml"][:, 17:18],
               S["sml"][:, 2:3])
        ctx.do("vector", "tensor_scalar", S["ftmp"][:, 8:16], S["io8"][:],
               S["sml"][:, 4:5], None, ALU.is_equal)
        ctx.do("vector", "tensor_scalar", S["ftmp"][:, 16:24], S["io8"][:],
               S["sml"][:, 5:6], None, ALU.is_equal)
        ctx.do("vector", "drain")
        ctx.do("vector", "tensor_scalar",
               S["call"][:, tc * E:(tc + 1) * E], S["ftmp"][:, 8:16],
               S["sml"][:, 6:7], None, ALU.mult)
        ctx.do("vector", "drain")
        ctx.do("vector", "scalar_tensor_tensor",
               S["call"][:, tc * E:(tc + 1) * E], S["ftmp"][:, 16:24],
               S["sml"][:, 7:8], S["call"][:, tc * E:(tc + 1) * E],
               ALU.mult, ALU.add, inc="dv")
    st["call_ready"] = ctx.tok("dv")
    st["rtr_pe"] = ctx.tok("pe")
    dump_buf(ctx, st, "c0_dbg", S["call"][:], st["call_ready"])


def emit_moe0(ctx, st):
    S = ctx.sb
    bank = ctx.bank
    dyn = S["dyn"]
    HB, WI, WG, WO = 0, 32768, 34816, 36864

    def dsl(off, n):
        return dyn[:, off:off + n]

    bi_m = ctx.do("sync", "dma_start", out=S["bib"][:], in_=ctx.dr["bi0"],
                  inc="sy", incn=16)
    bg_m = ctx.do("sync", "dma_start", out=S["bgh"][:], in_=ctx.dr["bg0"],
                  inc="sy", incn=16)
    ctx.wait("vector", bg_m)
    ctx.do("vector", "tensor_scalar", S["bgh"][:], S["bgh"][:], 0.5, None,
           ALU.mult, inc="dv")
    bgh_m = ctx.tok("dv")

    wi3 = ctx.dr["wi0"]
    wg3 = ctx.dr["wg0"]
    wo3 = ctx.dr["wo0"]
    ctx.wait("gpsimd", st.get("dyn_pe_free"), st.get("dyn_dv_free"))
    slot_free = {}
    bank_free = {}
    tmp_free = {}
    gb_free = None
    x_marks = []
    hseq = {}

    for e in range(MOE0_ESTART, MOE0_NE):
        if MOE0_BAR & 1:
            ctx.barrier()
        ctx.wait("sync", gb_free)
        gm = ctx.do("sync", "dma_start", out=S["gamb"][:],
                    in_=ctx.dr["gam0"][e], inc="sy", incn=16)
        bm = ctx.do("sync", "dma_start", out=S["betb"][:],
                    in_=ctx.dr["bet0"][e], inc="sy", incn=16)
        ctx.wait("gpsimd", st.get("bo_pe_free"))
        bo_m = dmaw(ctx, S["borow"][0:1, :], ctx.dr["bo0"][e:e + 1, :])
        for th in range(MOE0_NTH):
            if MOE0_BAR & 2:
                ctx.barrier()

            def wl_ig(fc):
                par = fc % 2
                ctx.wait("gpsimd", slot_free.get(("wi", par)))
                wim = dmaw(ctx, dsl(WI + par * 1024, 1024)
                           .rearrange("p (c m) -> p c m", c=CC),
                           wi3[e, fc])
                wgm = dmaw(ctx, dsl(WG + par * 1024, 1024)
                           .rearrange("p (c m) -> p c m", c=CC),
                           wg3[e, fc])
                return wim, wgm

            wl = {0: wl_ig(0)}
            for fc in range(NF0):
                par = fc % 2
                if fc + 1 < NF0:
                    wl[fc + 1] = wl_ig(fc + 1)
                wim, wgm = wl.pop(fc)
                for hf in range(2):
                    q = fc * 2 + hf
                    hb = (0, 2)[q % 2]
                    gb = (1, 3)[q % 2]
                    ctx.wait("tensor", wim, wgm, bank_free.get(hb),
                             bank_free.get(gb),
                             st["xn_ready"] if (e, th, fc, hf) == (0, 0, 0, 0)
                             else None,
                             bgh_m if (e, th, fc, hf) == (0, 0, 0, 0) else None)
                    for cc in range(CC):
                        ctx.do("tensor", "matmul", bank(hb),
                               dsl(WI + par * 1024 + cc * 128, 128),
                               xnf(S, cc, th * 1024 + hf * 512, 512),
                               start=(cc == 0), stop=(cc == CC - 1))
                    for cc in range(CC):
                        ctx.do("tensor", "matmul", bank(gb),
                               dsl(WG + par * 1024 + cc * 128, 128),
                               xnf(S, cc, th * 1024 + hf * 512, 512),
                               start=(cc == 0), stop=(cc == CC - 1),
                               inc="pe" if cc == CC - 1 else None)
                    mm_q = ctx.tok("pe")
                    if hf == 1:
                        slot_free[("wi", par)] = mm_q
                    tslot = q % 2
                    bcol = S["bib"][:, e * NF0 + fc:e * NF0 + fc + 1]
                    gcol = S["bgh"][:, e * NF0 + fc:e * NF0 + fc + 1]
                    ctx.wait("scalar", mm_q, tmp_free.get(tslot))
                    ctx.do("scalar", "activation",
                           S["tmp"][:, tslot * 512:(tslot + 1) * 512],
                           bank(hb), AF.Gelu, bias=bcol, scale=1.0)
                    ctx.do("scalar", "activation",
                           S["tmp"][:, 1024 + tslot * 512:1536 + tslot * 512],
                           bank(gb), AF.Tanh, bias=gcol, scale=0.5, inc="ac")
                    ev_q = ctx.tok("ac")
                    bank_free[hb] = ev_q
                    bank_free[gb] = ev_q
                    ctx.wait("vector", ev_q)
                    ctx.do("vector", "tensor_scalar",
                           S["tmp"][:, 1024 + tslot * 512:1536 + tslot * 512],
                           S["tmp"][:, 1024 + tslot * 512:1536 + tslot * 512],
                           0.5, 0.5, ALU.mult, ALU.add)
                    prev = hseq.get("prev")
                    if prev is not None:
                        pe_, pth_, pfc_, phf_ = prev
                        pq = pfc_ * 2 + phf_
                        pts = pq % 2
                        ctx.do("vector", "tensor_mul",
                               dsl(HB + pfc_ * 1024 + phf_ * 512, 512),
                               S["tmp"][:, pts * 512:(pts + 1) * 512],
                               S["tmp"][:, 1024 + pts * 512:1536 + pts * 512],
                               inc="dv")
                        tmp_free[pts] = ctx.tok("dv")
                    hseq["prev"] = (e, th, fc, hf)
            pe_, pth_, pfc_, phf_ = hseq["prev"]
            pq = pfc_ * 2 + phf_
            pts = pq % 2
            ctx.do("vector", "tensor_mul",
                   dsl(HB + pfc_ * 1024 + phf_ * 512, 512),
                   S["tmp"][:, pts * 512:(pts + 1) * 512],
                   S["tmp"][:, 1024 + pts * 512:1536 + pts * 512], inc="dv")
            tmp_free[pts] = ctx.tok("dv")
            hseq["prev"] = None
            h_ready = ctx.tok("dv")
            if MOE0_BAR & 8:
                ctx.barrier()

            for grp in range(MOE0_NGRP):
                if MOE0_BAR & 4:
                    ctx.barrier()
                grp_marks = {}
                ctx.wait("tensor", h_ready,
                         *[bank_free.get(b) for b in range(8)])
                for j in range(4):
                    tc8 = grp * 4 + j
                    for half in range(2):
                        b = 2 * j + half
                        for ci in range(4):
                            cc = half * 4 + ci
                            ctx.do("tensor", "matmul",
                                   bank(b)[:, ci * 128:(ci + 1) * 128],
                                   xnf(S, cc, th * 1024 + tc8 * 128, 128),
                                   S["cst"][:],
                                   start=(ci == 0), stop=False)
                for fc in range(NF0):
                    par = fc % 2
                    ctx.wait("gpsimd", slot_free.get(("wo", par)))
                    wom = dmaw(ctx, dsl(WO + par * 1024, 1024), wo3[e, fc])
                    ctx.wait("tensor", wom)
                    for j in range(4):
                        tc8 = grp * 4 + j
                        for half in range(2):
                            b = 2 * j + half
                            ctx.do("tensor", "matmul", bank(b),
                                   dsl(HB + fc * 1024 + tc8 * 128, 128),
                                   dsl(WO + par * 1024 + half * 512, 512),
                                   start=False, stop=False,
                                   inc="pe" if (j, half) == (3, 1) else None)
                    slot_free[("wo", par)] = ctx.tok("pe")
                ctx.wait("tensor", bo_m)
                for j in range(4):
                    for half in range(2):
                        b = 2 * j + half
                        ctx.do("tensor", "matmul", bank(b), S["one1"][:],
                               S["borow"][0:1, half * 512:(half + 1) * 512],
                               start=False, stop=True,
                               inc="pe" if half == 1 else None)
                    grp_marks[j] = ctx.tok("pe")
                for j in range(4):
                    tc8 = grp * 4 + j
                    tc = th * 8 + tc8
                    ctx.wait("vector", grp_marks[j], gm, bm,
                             st["call_ready"] if (e, th, grp, j) == (0, 0, 0, 0)
                             else None)
                    for half in range(2):
                        b = 2 * j + half
                        ctx.do("vector", "tensor_copy",
                               S["rt"][:, half * 512:(half + 1) * 512],
                               bank(b), inc="dv" if half == 1 else None)
                    bank_free[2 * j] = ctx.tok("dv")
                    bank_free[2 * j + 1] = ctx.tok("dv")
                    ctx.do("vector", "drain")
                    ctx.do("vector", "bn_stats", S["sml"][:, 0:6],
                           S["rt"][:, 0:512])
                    ctx.do("vector", "bn_stats", S["sml"][:, 6:12],
                           S["rt"][:, 512:1024])
                    ctx.do("vector", "drain")
                    ctx.do("vector", "bn_aggr", S["sml"][:, 12:14],
                           S["sml"][:, 0:12], inc="dv")
                    ctx.wait("scalar", ctx.tok("dv"))
                    ctx.do("scalar", "activation", S["sml"][:, 14:15],
                           S["sml"][:, 13:14], AF.Sqrt,
                           bias=S["epss"][:, 1:2], scale=1.0, inc="ac")
                    ctx.wait("vector", ctx.tok("ac"))
                    ctx.do("vector", "reciprocal", S["sml"][:, 15:16],
                           S["sml"][:, 14:15])
                    ctx.do("vector", "drain")
                    ctx.do("vector", "tensor_mul", S["sml"][:, 16:17],
                           S["call"][:, tc * E + e:tc * E + e + 1],
                           S["sml"][:, 15:16])
                    ctx.do("vector", "drain")
                    ctx.do("vector", "tensor_mul", S["sml"][:, 17:18],
                           S["sml"][:, 12:13], S["sml"][:, 16:17])
                    ctx.do("vector", "drain")
                    ctx.do("vector", "tensor_scalar", S["sml"][:, 17:18],
                           S["sml"][:, 17:18], -1.0, None, ALU.mult)
                    ctx.do("vector", "drain")
                    ctx.do("vector", "tensor_scalar", S["ftmp"][:],
                           S["rt"][:], S["sml"][:, 16:17], S["sml"][:, 17:18],
                           ALU.mult, ALU.add)
                    ctx.do("vector", "drain")
                    ctx.do("vector", "tensor_mul", S["ftmp"][:],
                           S["ftmp"][:], S["gamb"][:])
                    ctx.do("vector", "drain")
                    ctx.do("vector", "scalar_tensor_tensor", S["ftmp"][:],
                           S["betb"][:],
                           S["call"][:, tc * E + e:tc * E + e + 1],
                           S["ftmp"][:], ALU.mult, ALU.add)
                    ctx.do("vector", "drain")
                    ctx.do("vector", "tensor_add", xt(S, tc), xt(S, tc),
                           S["ftmp"][:], inc="dv")
                    x_marks.append(ctx.tok("dv"))
        gb_free = ctx.tok("dv")
        st["bo_pe_free"] = ctx.tok("pe")
    st["x_writes"] = [x_marks[-1]]
    st["xn_consumers"].append(ctx.tok("pe"))
    st["dyn_pe_free"] = ctx.tok("pe")
    dump_buf(ctx, st, "x_m0", S["x"][:], x_marks[-1])
    dump_buf(ctx, st, "h_dbg", S["dyn"][:, 0:32768], x_marks[-1])
    dump_buf(ctx, st, "r_dbg", S["rt"][:], x_marks[-1])





def emit_router1(ctx, st):
    S = ctx.sb
    bank = ctx.bank
    gw_m = ctx.do("sync", "dma_start",
                  out=S["wrf"][:].rearrange("p (c e) -> p c e", e=E),
                  in_=ctx.dr["gw1"].rearrange("c p e -> p c e"),
                  inc="sy", incn=16)
    ctx.wait("tensor", gw_m, st["stats_ready"])
    lg_done = None
    tr_done = None
    ev_done = None
    cnt_done = None
    for tc in range(NT):
        b = 4 + tc % 2
        ctx.wait("vector", tr_done)
        ctx.do("vector", "tensor_scalar", S["ftmp"][:], xt(S, tc),
               S["lnst"][:, tc:tc + 1], S["lnst"][:, 16 + tc:17 + tc],
               ALU.subtract, ALU.mult, inc="dv")
        mx = ctx.tok("dv")
        ctx.wait("tensor", mx, ev_done, tr_done)
        for j in range(CC):
            ctx.do("tensor", "transpose",
                   bank(j // 4)[:, (j % 4) * 128:(j % 4 + 1) * 128],
                   S["ftmp"][:, j * 128:(j + 1) * 128], S["idnfs"][:],
                   inc="pe" if j == CC - 1 else None)
        mtr = ctx.tok("pe")
        ctx.wait("vector", mtr)
        ctx.do("vector", "tensor_copy", S["rt"][:, 0:512], bank(0))
        ctx.do("vector", "tensor_copy", S["rt"][:, 512:1024], bank(1),
               inc="dv")
        tr_done = ctx.tok("dv")
        ctx.wait("tensor", tr_done, lg_done)
        for cc in range(CC):
            ctx.do("tensor", "matmul", bank(b)[:, 0:E],
                   S["rt"][:, cc * 128:(cc + 1) * 128],
                   S["wrf"][:, cc * E:(cc + 1) * E],
                   start=(cc == 0), stop=(cc == CC - 1), inc="pe" if cc == CC - 1 else None)
        ev_done = ctx.tok("pe")
        ml = ev_done
        ctx.wait("vector", ml)
        ctx.do("vector", "reduce_max", S["sml"][:, 0:1], bank(b)[:, 0:E],
               AX.X, negate=True, inc="dv")
        ctx.wait("scalar", ctx.tok("dv"))
        ctx.do("scalar", "activation", S["ftmp"][:, 0:E], bank(b)[:, 0:E],
               AF.Exp, bias=S["sml"][:, 0:1], scale=1.0,
               accum_out=S["sml"][:, 3:4], inc="ac")
        lg_done = ctx.tok("ac")
        ctx.wait("vector", lg_done)
        ctx.do("vector", "max", S["sml"][:, 16:24], S["ftmp"][:, 0:E])
        ctx.do("vector", "reciprocal", S["sml"][:, 2:3], S["sml"][:, 3:4])
        ctx.do("vector", "max_index", S["sml"][:, 24:32].bitcast(U32),
               S["sml"][:, 16:24], S["ftmp"][:, 0:E])
        ctx.do("vector", "drain")
        ctx.do("vector", "tensor_scalar", S["sml"][:, 8:10], S["sml"][:, 16:18],
               S["sml"][:, 2:3], None, ALU.mult)
        ctx.do("vector", "tensor_copy", S["sml"][:, 4:6],
               S["sml"][:, 24:32].bitcast(U32)[:, 0:2])
        ctx.do("vector", "drain")
        ctx.do("vector", "tensor_add", S["sml"][:, 1:2], S["sml"][:, 8:9],
               S["sml"][:, 9:10])
        ctx.do("vector", "drain")
        ctx.do("vector", "tensor_scalar", S["sml"][:, 1:2], S["sml"][:, 1:2],
               1e-9, None, ALU.add)
        ctx.do("vector", "drain")
        ctx.do("vector", "reciprocal", S["sml"][:, 2:3], S["sml"][:, 1:2])
        ctx.do("vector", "drain")
        ctx.do("vector", "tensor_scalar", S["wn12"][:, tc * 2:tc * 2 + 2],
               S["sml"][:, 8:10], S["sml"][:, 2:3], None, ALU.mult)
        ctx.do("vector", "tensor_scalar", S["m12"][:, tc * 16:tc * 16 + 8],
               S["io8"][:], S["sml"][:, 4:5], None, ALU.is_equal)
        ctx.do("vector", "tensor_scalar", S["m12"][:, tc * 16 + 8:tc * 16 + 16],
               S["io8"][:], S["sml"][:, 5:6], None, ALU.is_equal, inc="dv")
        mm12 = ctx.tok("dv")
        ctx.wait("tensor", mm12, cnt_done if tc == 0 else None)
        ctx.do("tensor", "matmul", bank(2)[0:1, 0:16], S["onef"][:],
               S["m12"][:, tc * 16:(tc + 1) * 16],
               start=(tc == 0), stop=(tc == NT - 1),
               inc="pe" if tc == NT - 1 else None)
    mcnt = ctx.tok("pe")
    ctx.wait("vector", mcnt)
    ctx.do("vector", "tensor_copy", S["carry"][:], bank(2)[0:1, 0:16],
           inc="dv")
    ctx.wait("sync", ctx.tok("dv"))
    ctx.do("sync", "dma_start", out=ctx.dr["r_in"], in_=S["carry"][:],
           inc="sy", incn=16)
    ctx.wait("gpsimd", ctx.tok("sy"))
    ctx.do("gpsimd", "collective_compute", "AllGather", ALU.bypass,
           replica_groups=GROUPS_ALL,
           ins=[ctx.dr["r_in"]], outs=[ctx.dr["r_out"]], inc="c2")
    ctx.wait("sync", ctx.tok("c2"))
    ctx.do("sync", "dma_start", out=S["rowt"][:], in_=ctx.dr["r_out"],
           inc="sy", incn=16)
    rw_m = ctx.tok("sy")
    ctx.wait("tensor", rw_m)
    ctx.do("tensor", "matmul", bank(3)[:, 0:16], S["omask"][:], S["rowt"][:],
           start=True, stop=True, inc="pe")
    moff = ctx.tok("pe")
    ctx.wait("vector", moff)
    ctx.do("vector", "tensor_copy", S["cnt16"][:], bank(3)[:, 0:16])
    ctx.do("vector", "drain")
    ctx.do("vector", "tensor_copy", S["carry"][:], S["cnt16"][0:1, :],
           inc="dv")
    carry_m = ctx.tok("dv")

    rank_done = None
    for tc in range(NT):
        b = 2 + tc % 2
        ctx.wait("tensor", carry_m if tc == 0 else None, rank_done)
        ctx.do("tensor", "matmul", bank(b)[:, 0:16], S["linc"][:],
               S["m12"][:, tc * 16:(tc + 1) * 16], start=True, stop=False)
        ctx.do("tensor", "matmul", bank(b)[:, 0:16], S["one1f"][:],
               S["carry"][:], start=False, stop=True)
        ctx.do("tensor", "matmul", bank(b)[0:1, 16:32], S["onef"][:],
               S["m12"][:, tc * 16:(tc + 1) * 16],
               start=True, stop=True, inc="pe")
        mrank = ctx.tok("pe")
        ctx.wait("vector", mrank)
        ctx.do("vector", "tensor_add", S["carry"][:], S["carry"][:],
               bank(b)[0:1, 16:32])
        ctx.do("vector", "tensor_mul", S["ftmp"][:, 0:16],
               bank(b)[:, 0:16], S["m12"][:, tc * 16:(tc + 1) * 16],
               inc="dv")
        rank_done = ctx.tok("dv")
        ctx.do("vector", "drain")
        ctx.do("vector", "reduce_sum", S["sml"][:, 0:1], S["ftmp"][:, 0:8],
               AX.X)
        ctx.do("vector", "reduce_sum", S["sml"][:, 1:2], S["ftmp"][:, 8:16],
               AX.X)
        ctx.do("vector", "drain")
        ctx.do("vector", "tensor_scalar", S["sml"][:, 2:4], S["sml"][:, 0:2],
               CAP, None, ALU.is_le)
        ctx.do("vector", "drain")
        ctx.do("vector", "tensor_mul", S["sml"][:, 4:6],
               S["wn12"][:, tc * 2:tc * 2 + 2], S["sml"][:, 2:4])
        ctx.do("vector", "drain")
        ctx.do("vector", "tensor_scalar",
               S["c1al"][:, tc * E:(tc + 1) * E],
               S["m12"][:, tc * 16:tc * 16 + 8],
               S["sml"][:, 4:5], None, ALU.mult)
        ctx.do("vector", "drain")
        ctx.do("vector", "scalar_tensor_tensor",
               S["c1al"][:, tc * E:(tc + 1) * E],
               S["m12"][:, tc * 16 + 8:tc * 16 + 16],
               S["sml"][:, 5:6], S["c1al"][:, tc * E:(tc + 1) * E],
               ALU.mult, ALU.add, inc="dv")
    st["c1_ready"] = ctx.tok("dv")
    dump_buf(ctx, st, "c1_dbg", S["c1al"][:], st["c1_ready"])



def phi_evac(ctx, psum_ap, bias_col, out_bf, fbuf):
    S = ctx.sb
    ctx.do("vector", "tensor_scalar", fbuf[:, 0:512], psum_ap,
           bias_col, 0.0, ALU.add, ALU.min, inc="dv")
    ctx.wait("scalar", ctx.tok("dv"))
    ctx.do("scalar", "activation", S["rt"][:, 0:512], fbuf[:, 0:512],
           AF.Exp, inc="ac")
    ctx.do("vector", "tensor_scalar", fbuf[:, 512:1024], psum_ap,
           bias_col, 0.0, ALU.add, ALU.max)
    ctx.wait("vector", ctx.tok("ac"))
    ctx.do("vector", "tensor_add", out_bf, fbuf[:, 512:1024],
           S["rt"][:, 0:512], inc="dv")
    return ctx.tok("dv")


def emit_attn1(ctx, st):
    S = ctx.sb
    bank, bankb = ctx.bank, ctx.bankb
    dyn = S["dyn"]

    def dsl(off, n):
        return dyn[:, off:off + n]

    QG, KG, VG = 0, 4096, 8192
    WQS, WKS, WVS, WOS = 12288, 14336, 16384, 24576
    KTM = 32768
    ATM = 33024
    AFM = 35072
    ctx.wait("gpsimd", st.get("dyn_pe_free"), st.get("dyn_dv_free"))
    bq_m = ctx.do("sync", "dma_start", out=S["bsm"][:, 0:CC],
                  in_=ctx.dr["bq1"], inc="sy", incn=16)
    bk_m = ctx.do("sync", "dma_start", out=S["bsm"][:, CC:2 * CC],
                  in_=ctx.dr["bk1"], inc="sy", incn=16)
    bv_m = dmaw(ctx, S["bvrow"][0:1, :], ctx.dr["bv1row"])
    bo_m = dmaw(ctx, S["borow"][0:1, :], ctx.dr["bo1"])
    wv_m = dmaw(ctx, dsl(WVS, 8192).rearrange("p (c m) -> p c m", c=CC),
                ctx.dr["wv1"].rearrange("c p m -> p c m"))
    ctx.wait("vector", st.get("Ssb_free"), bq_m, bk_m)
    ctx.do("vector", "memset", S["Ssb"][:], 0.0, inc="dv")
    s_zero = ctx.tok("dv")

    xn_rdy = st["xn_ready"]
    prod_banks = [0, 1]
    pd = {}

    def produce_group(g, what, phi, wslot, wdram, bias_idx, dst):
        if what == "fm":
            def wl_p(oc):
                par = oc % 2
                ctx.wait("gpsimd", pd.get(("ws", wslot, par)))
                return dmaw(ctx, dsl(wslot + par * 1024, 1024)
                            .rearrange("p (c m) -> p c m", c=CC),
                            wdram[oc])

            wlp = {0: wl_p(0)}
            for oc in range(CC):
                par = oc % 2
                if oc + 1 < CC:
                    wlp[oc + 1] = wl_p(oc + 1)
                wm = wlp.pop(oc)
                b = prod_banks[oc % 2]
                ctx.wait("tensor", wm, xn_rdy if g == 0 and oc == 0 else None,
                         pd.get(("pb", b)))
                for cc in range(CC):
                    ctx.do("tensor", "matmul", bank(b),
                           dsl(wslot + par * 1024 + cc * 128, 128),
                           xnf(S, cc, g * 512, 512),
                           start=(cc == 0), stop=(cc == CC - 1),
                           inc="pe" if cc == CC - 1 else None)
                mq = ctx.tok("pe")
                pd[("ws", wslot, par)] = mq
                ctx.wait("vector", mq)
                m = phi_evac(ctx, bank(b),
                             S["bsm"][:, bias_idx * CC + oc:bias_idx * CC + oc + 1],
                             dsl(dst + oc * 512, 512), S["ftmp"])
                pd[("pb", b)] = m
            return m
        else:
            for i in range(4):
                for half in range(2):
                    b = prod_banks[(i * 2 + half) % 2]
                    ctx.wait("tensor", wv_m, bv_m, pd.get(("pb", b)),
                             xn_rdy if g == 0 and i == 0 else None)
                    for cc in range(CC):
                        ctx.do("tensor", "matmul",
                               bank(b)[:, 0:512],
                               xnf(S, cc, g * 512 + i * 128, 128),
                               dsl(WVS + cc * 1024 + half * 512, 512),
                               start=(cc == 0), stop=False)
                    ctx.do("tensor", "matmul", bank(b)[:, 0:512], S["one1"][:],
                           S["bvrow"][0:1, half * 512:(half + 1) * 512],
                           start=False, stop=True,
                           inc="pe")
                    mq = ctx.tok("pe")
                    ctx.wait("vector", mq)
                    ctx.do("vector", "tensor_copy",
                           dsl(VG + i * 1024 + half * 512, 512),
                           bank(b)[:, 0:512], inc="dv")
                    pd[("pb", b)] = ctx.tok("dv")
            return ctx.tok("dv")

    tr_par = [None, None]
    sd_done = {}
    for g in range(4):
        produce_group(g, "fm", True, WKS, ctx.dr["wk1"], 1, KG)
        kv_rdy = produce_group(g, "tm", False, None, None, None, VG)
        ctx.wait("tensor", kv_rdy)
        for ci in range(4):
            for h in range(16):
                b64 = (h % 2) * 64
                par = h % 2
                pt = bankb(6 + par)
                ctx.wait("tensor", tr_par[par])
                ctx.do("tensor", "transpose", pt[:, 0:64],
                       dyn[b64:b64 + 64,
                           KG + (h // 2) * 512 + ci * 128:
                           KG + (h // 2) * 512 + ci * 128 + 128],
                       S["cst"][b64:b64 + 64, b64:b64 + 64],
                       tile_position=(b64, 0), inc="pe")
                mt = ctx.tok("pe")
                ctx.wait("vector", mt)
                ctx.do("vector", "tensor_copy",
                       dsl(KTM + par * 128, 64), pt[:, 0:64], inc="dv")
                tr_par[par] = ctx.tok("dv")
                bsd = 2 + (h % 2)
                ctx.wait("tensor", tr_par[par], sd_done.get(bsd))
                ctx.do("tensor", "matmul",
                       bank(bsd)[b64:b64 + 64, 0:64],
                       dsl(KTM + par * 128, 64),
                       dsl(VG + ci * 1024 + h * 64, 64),
                       start=True, stop=True, tile_position=(0, b64))
                ctx.do("tensor", "matmul",
                       bank(bsd)[b64:b64 + 64, 64:65],
                       dsl(KTM + par * 128, 64), S["onec"][:],
                       start=True, stop=True, tile_position=(0, b64),
                       inc="pe")
                msd = ctx.tok("pe")
                ctx.wait("vector", msd)
                ctx.do("vector", "tensor_add",
                       S["Ssb"][b64:b64 + 64, (h // 2) * 64:(h // 2) * 64 + 64],
                       S["Ssb"][b64:b64 + 64, (h // 2) * 64:(h // 2) * 64 + 64],
                       bank(bsd)[b64:b64 + 64, 0:64])
                ctx.do("vector", "tensor_add",
                       S["Ssb"][b64:b64 + 64, 512 + h // 2:513 + h // 2],
                       S["Ssb"][b64:b64 + 64, 512 + h // 2:513 + h // 2],
                       bank(bsd)[b64:b64 + 64, 64:65], inc="dv")
                sd_done[bsd] = ctx.tok("dv")
    s_local = ctx.tok("dv")
    dump_buf(ctx, st, "st_dbg", S["Ssb"][:], s_local)

    ctx.wait("sync", s_local, s_zero)
    ctx.do("sync", "dma_start", out=ctx.dr["s_in"], in_=S["Ssb"][:],
           inc="sy", incn=16)
    ctx.wait("gpsimd", ctx.tok("sy"))
    ctx.do("gpsimd", "collective_compute", "AllGather", ALU.bypass,
           replica_groups=GROUPS_ALL,
           ins=[ctx.dr["s_in"]], outs=[ctx.dr["s_out"]], inc="c1")
    cm_ = ctx.tok("c1")
    ctx.wait("vector", cm_)
    ctx.do("vector", "memset", S["Ssb"][:], 0.0, inc="dv")
    so = ctx.dr["s_out"].rearrange("(r p) c -> r p c", p=128)
    for r in range(N_CORES):
        lm = ctx.do("sync", "dma_start", out=S["gath"][:], in_=so[r],
                    waits=[cm_], inc="sy", incn=16)
        ctx.wait("vector", lm)
        ctx.do("vector", "scalar_tensor_tensor", S["Ssb"][:], S["gath"][:],
               S["pmask"][:, r:r + 1], S["Ssb"][:], ALU.mult, ALU.add,
               inc="dv")
        ctx.do("vector", "drain")
        ctx.wait("sync", ctx.tok("dv"))
    ctx.do("vector", "tensor_copy", S["Sbf"][:], S["Ssb"][:], inc="dv")
    s_init = ctx.tok("dv")

    wo_m = None
    x_marks = []
    ob = {}
    ctx.wait("tensor", s_init)
    for g in range(4):
        produce_group(g, "fm", True, WQS, ctx.dr["wq1"], 0, QG)
        produce_group(g, "fm", True, WKS, ctx.dr["wk1"], 1, KG)
        kv_rdy = produce_group(g, "tm", False, None, None, None, VG)
        if g == 0:
            ctx.wait("gpsimd", ctx.tok("pe"))
            wo_m = dmaw(ctx, dsl(WOS, 8192).rearrange("p (c m) -> p c m", c=CC),
                        ctx.dr["wo1"].rearrange("c p m -> p c m"))
        for ci in range(4):
            tc = g * 4 + ci
            par = tc % 2
            atm = dsl(ATM + par * 1024, 1024)
            afm = dsl(AFM + par * 1024, 1024)
            for h in range(16):
                b64 = (h % 2) * 64
                hb2 = h // 2
                qsl = dyn[b64:b64 + 64,
                          QG + hb2 * 512 + ci * 128:QG + hb2 * 512 + ci * 128 + 128]
                ksl = dyn[b64:b64 + 64,
                          KG + hb2 * 512 + ci * 128:KG + hb2 * 512 + ci * 128 + 128]
                vsl = dsl(VG + ci * 1024 + h * 64, 64)
                tb = (h % 2) * 256
                bsT = 2 + (h % 2)
                bs = 4 + (h % 2)
                ctx.wait("tensor", kv_rdy if h == 0 else None,
                         ob.get(("sT", bsT)), ob.get(("s", bs)),
                         sd_done.get(bsT), ob.get(("sbf",)) if h == 0 else None)
                ctx.do("tensor", "matmul", bank(bsT)[:, 0:128], ksl, qsl,
                       start=True, stop=True, tile_position=(b64, 0))
                ctx.do("tensor", "matmul", bank(bsT)[:, 128:256], qsl, ksl,
                       start=True, stop=True, tile_position=(b64, 0))
                ctx.do("tensor", "matmul", bank(bsT)[:, 256:257], qsl,
                       S["Sbf"][b64:b64 + 64, 512 + hb2:513 + hb2],
                       start=True, stop=True, tile_position=(b64, 0),
                       inc="pe")
                msc = ctx.tok("pe")
                ctx.wait("vector", msc, ob.get(("tmp", h % 2)))
                ctx.do("vector", "tensor_mul", S["tmp"][:, tb:tb + 128],
                       bank(bsT)[:, 0:128], S["mkq"][:])
                ctx.do("vector", "tensor_mul", S["tmp"][:, tb + 128:tb + 256],
                       bank(bsT)[:, 128:256], S["mqk"][:], inc="dv")
                ob[("sT", bsT)] = ctx.tok("dv")
                mmask = ctx.tok("dv")
                ctx.do("vector", "drain")
                ctx.do("vector", "reduce_sum", S["sml"][:, 0:1],
                       S["tmp"][:, tb + 128:tb + 256], AX.X)
                ctx.do("vector", "drain")
                ctx.do("vector", "tensor_add", S["sml"][:, 1:2],
                       S["sml"][:, 0:1], bank(bsT)[:, 256:257])
                ctx.do("vector", "drain")
                ctx.do("vector", "reciprocal", S["sml"][:, 2:3],
                       S["sml"][:, 1:2], inc="dv")
                mden = ctx.tok("dv")
                ctx.wait("tensor", mmask)
                ctx.do("tensor", "matmul", bank(bs)[:, 0:64],
                       S["tmp"][:, tb:tb + 128], vsl, start=True, stop=False)
                ctx.do("tensor", "matmul", bank(bs)[:, 0:64], qsl,
                       S["Sbf"][b64:b64 + 64, hb2 * 64:hb2 * 64 + 64],
                       start=False, stop=True, tile_position=(b64, 0),
                       inc="pe")
                mnum = ctx.tok("pe")
                ctx.wait("vector", mnum)
                ctx.do("vector", "tensor_scalar", atm[:, h * 64:h * 64 + 64],
                       bank(bs)[:, 0:64], S["sml"][:, 2:3], None, ALU.mult,
                       inc="dv")
                ob[("s", bs)] = ctx.tok("dv")
                ob[("tmp", h % 2)] = mnum
            ctx.wait("tensor", ctx.tok("dv"))
            for h in range(16):
                b64 = (h % 2) * 64
                hb2 = h // 2
                parh = h % 2
                pt = bankb(6 + parh)
                ctx.wait("tensor", tr_par[parh])
                ctx.do("tensor", "transpose", pt[:, 0:64],
                       dyn[b64:b64 + 64,
                           KG + hb2 * 512 + ci * 128:KG + hb2 * 512 + ci * 128 + 128],
                       S["cst"][b64:b64 + 64, b64:b64 + 64],
                       tile_position=(b64, 0), inc="pe")
                mt = ctx.tok("pe")
                ctx.wait("vector", mt)
                ctx.do("vector", "tensor_copy",
                       dsl(KTM + parh * 128, 64), pt[:, 0:64], inc="dv")
                tr_par[parh] = ctx.tok("dv")
                bsd = 2 + parh
                ctx.wait("tensor", tr_par[parh], sd_done.get(bsd))
                ctx.do("tensor", "matmul",
                       bank(bsd)[b64:b64 + 64, 300:364],
                       dsl(KTM + parh * 128, 64),
                       dsl(VG + ci * 1024 + h * 64, 64),
                       start=True, stop=True, tile_position=(0, b64))
                ctx.do("tensor", "matmul",
                       bank(bsd)[b64:b64 + 64, 364:365],
                       dsl(KTM + parh * 128, 64), S["onec"][:],
                       start=True, stop=True, tile_position=(0, b64),
                       inc="pe")
                msd = ctx.tok("pe")
                ctx.wait("vector", msd)
                ctx.do("vector", "tensor_add",
                       S["Ssb"][b64:b64 + 64, hb2 * 64:hb2 * 64 + 64],
                       S["Ssb"][b64:b64 + 64, hb2 * 64:hb2 * 64 + 64],
                       bank(bsd)[b64:b64 + 64, 300:364])
                ctx.do("vector", "tensor_add",
                       S["Ssb"][b64:b64 + 64, 512 + hb2:513 + hb2],
                       S["Ssb"][b64:b64 + 64, 512 + hb2:513 + hb2],
                       bank(bsd)[b64:b64 + 64, 364:365], inc="dv")
                sd_done[bsd] = ctx.tok("dv")
            ctx.do("vector", "drain")
            ctx.do("vector", "tensor_copy", S["Sbf"][:], S["Ssb"][:],
                   inc="dv")
            sbf_m = ctx.tok("dv")
            ob[("sbf",)] = sbf_m
            matm = ctx.tok("dv")
            pt = bankb(6 + par)
            ctx.wait("tensor", matm, tr_par[par])
            for j in range(CC):
                ctx.do("tensor", "transpose", pt[:, j * 128:(j + 1) * 128],
                       atm[:, j * 128:(j + 1) * 128], S["cst"][:],
                       inc="pe" if j == CC - 1 else None)
            mtr = ctx.tok("pe")
            tr_par[par] = None
            ctx.wait("vector", mtr)
            for j in range(CC):
                ctx.do("vector", "tensor_copy", afm[:, j * 128:(j + 1) * 128],
                       pt[:, j * 128:(j + 1) * 128],
                       inc="dv" if j == CC - 1 else None)
            maf = ctx.tok("dv")
            tr_par[par] = maf
            ctx.wait("tensor", maf, wo_m, bo_m, ob.get(("proj",)))
            for half in range(2):
                bb = (0, 1)[half]
                ctx.wait("tensor", pd.get(("pb", bb)))
                for j in range(CC):
                    ctx.do("tensor", "matmul", bank(bb),
                           afm[:, j * 128:(j + 1) * 128],
                           dsl(WOS + j * 1024 + half * 512, 512),
                           start=(j == 0), stop=False)
                ctx.do("tensor", "matmul", bank(bb), S["one1"][:],
                       S["borow"][0:1, half * 512:(half + 1) * 512],
                       start=False, stop=True, inc="pe")
            mpr = ctx.tok("pe")
            ctx.wait("vector", mpr)
            ctx.do("vector", "tensor_add", xt(S, tc)[:, 0:512],
                   xt(S, tc)[:, 0:512], bank(0))
            ctx.do("vector", "tensor_add", xt(S, tc)[:, 512:1024],
                   xt(S, tc)[:, 512:1024], bank(1), inc="dv")
            mx_ = ctx.tok("dv")
            pd[("pb", 0)] = mx_
            pd[("pb", 1)] = mx_
            ob[("proj",)] = mx_
            x_marks.append(mx_)
    st["x_writes"] = [x_marks[-1]]
    st["xn_consumers"].append(ctx.tok("pe"))
    st["dyn_pe_free"] = ctx.tok("pe")
    st["dyn_dv_free"] = ctx.tok("dv")
    st["bo_pe_free"] = ctx.tok("pe")
    st["Ssb_free"] = ctx.tok("dv")
    dump_buf(ctx, st, "x_a1", S["x"][:], x_marks[-1])



def emit_moe1(ctx, st):
    S = ctx.sb
    bank = ctx.bank
    dyn = S["dyn"]
    GU, WG1, WU1, WD1 = 0, 32768, 34816, 36864

    def dsl(off, n):
        return dyn[:, off:off + n]

    ctx.wait("gpsimd", st.get("dyn_pe_free"), st.get("dyn_dv_free"))
    wg3 = ctx.dr["wg1"]
    wu3 = ctx.dr["wu1"]
    wd3 = ctx.dr["wd1"]
    slot_free = {}
    bank_free = {}
    tmp_free = {}
    x_marks = []
    prev = [None]

    for e in range(E):
        def wl_gu(hc):
            par = hc % 2
            ctx.wait("gpsimd", slot_free.get(("wg", par)))
            wgm = dmaw(ctx, dsl(WG1 + par * 1024, 1024)
                       .rearrange("p (c m) -> p c m", c=CC), wg3[e, hc])
            wum = dmaw(ctx, dsl(WU1 + par * 1024, 1024)
                       .rearrange("p (c m) -> p c m", c=CC), wu3[e, hc])
            return wgm, wum

        wl = {0: wl_gu(0)}
        for hc in range(NH1):
            par = hc % 2
            if hc + 1 < NH1:
                wl[hc + 1] = wl_gu(hc + 1)
            wgm, wum = wl.pop(hc)
            for tg in range(4):
                q = hc * 4 + tg
                gb = (0, 2)[q % 2]
                ub = (1, 3)[q % 2]
                ctx.wait("tensor", wgm, wum, bank_free.get(gb),
                         bank_free.get(ub),
                         st["xn_ready"] if (e, hc, tg) == (0, 0, 0) else None,
                         st.get("c1_ready") if (e, hc, tg) == (0, 0, 0) else None)
                for cc in range(CC):
                    ctx.do("tensor", "matmul", bank(gb),
                           dsl(WG1 + par * 1024 + cc * 128, 128),
                           xnf(S, cc, tg * 512, 512),
                           start=(cc == 0), stop=(cc == CC - 1))
                for cc in range(CC):
                    ctx.do("tensor", "matmul", bank(ub),
                           dsl(WU1 + par * 1024 + cc * 128, 128),
                           xnf(S, cc, tg * 512, 512),
                           start=(cc == 0), stop=(cc == CC - 1),
                           inc="pe" if cc == CC - 1 else None)
                mm_q = ctx.tok("pe")
                if tg == 3:
                    slot_free[("wg", par)] = mm_q
                ts_ = q % 2
                ctx.wait("scalar", mm_q, tmp_free.get(ts_))
                ctx.do("scalar", "activation",
                       S["tmp"][:, ts_ * 512:(ts_ + 1) * 512],
                       bank(gb), AF.Silu)
                ctx.do("scalar", "activation",
                       S["tmp"][:, 1024 + ts_ * 512:1536 + ts_ * 512],
                       bank(ub), AF.Copy, inc="ac")
                ev_q = ctx.tok("ac")
                bank_free[gb] = ev_q
                bank_free[ub] = ev_q
                ctx.wait("vector", ev_q)
                if prev[0] is not None:
                    phc, ptg = prev[0]
                    pts = (phc * 4 + ptg) % 2
                    ctx.do("vector", "tensor_mul",
                           dsl(GU + phc * 2048 + ptg * 512, 512),
                           S["tmp"][:, pts * 512:(pts + 1) * 512],
                           S["tmp"][:, 1024 + pts * 512:1536 + pts * 512],
                           inc="dv")
                    tmp_free[pts] = ctx.tok("dv")
                prev[0] = (hc, tg)
        phc, ptg = prev[0]
        pts = (phc * 4 + ptg) % 2
        ctx.do("vector", "tensor_mul",
               dsl(GU + phc * 2048 + ptg * 512, 512),
               S["tmp"][:, pts * 512:(pts + 1) * 512],
               S["tmp"][:, 1024 + pts * 512:1536 + pts * 512], inc="dv")
        tmp_free[pts] = ctx.tok("dv")
        prev[0] = None
        gu_ready = ctx.tok("dv")

        for grp in range(4):
            grp_marks = {}
            ctx.wait("tensor", gu_ready,
                     *[bank_free.get(b) for b in range(8)])
            def wl_d(hc):
                ctx.wait("gpsimd", slot_free.get(("wd", hc % 2)))
                return dmaw(ctx, dsl(WD1 + (hc % 2) * 1024, 1024), wd3[e, hc])

            wld = {0: wl_d(0)}
            for hc in range(NH1):
                par = hc % 2
                if hc + 1 < NH1:
                    wld[hc + 1] = wl_d(hc + 1)
                wdm = wld.pop(hc)
                ctx.wait("tensor", wdm)
                for j in range(4):
                    tc = grp * 4 + j
                    for half in range(2):
                        b = 2 * j + half
                        ctx.do("tensor", "matmul", bank(b),
                               dsl(GU + hc * 2048 + tc * 128, 128),
                               dsl(WD1 + par * 1024 + half * 512, 512),
                               start=(hc == 0), stop=(hc == NH1 - 1),
                               inc="pe" if (j, half) == (3, 1) else None)
                slot_free[("wd", par)] = ctx.tok("pe")
                if hc == NH1 - 1:
                    for j in range(4):
                        grp_marks[j] = ctx.tok("pe")
            for j in range(4):
                tc = grp * 4 + j
                ctx.wait("vector", grp_marks[j])
                for half in range(2):
                    b = 2 * j + half
                    ctx.do("vector", "scalar_tensor_tensor",
                           xt(S, tc)[:, half * 512:(half + 1) * 512],
                           bank(b),
                           S["c1al"][:, tc * E + e:tc * E + e + 1],
                           xt(S, tc)[:, half * 512:(half + 1) * 512],
                           ALU.mult, ALU.add, inc="dv" if half == 1 else None)
                bank_free[2 * j] = ctx.tok("dv")
                bank_free[2 * j + 1] = ctx.tok("dv")
                x_marks.append(ctx.tok("dv"))
    st["x_writes"] = [x_marks[-1]]
    st["xn_consumers"].append(ctx.tok("pe"))
    st["dyn_pe_free"] = ctx.tok("pe")
    st["dyn_dv_free"] = ctx.tok("dv")
    dump_buf(ctx, st, "x_m1", S["x"][:], x_marks[-1])


PHASE_EMITTERS = {
    "ln0": lambda c, s: emit_ln(c, s, 0),
    "ln1": lambda c, s: emit_ln(c, s, 1),
    "attn0": emit_attn0,
    "rt0": emit_router0,
    "moe0": emit_moe0,
    "rt1": emit_router1,
    "attn1": emit_attn1,
    "moe1": emit_moe1,
}



PLAN = [["ln0", "rt0", "attn0", "moe0", "ln1", "rt1", "attn1", "moe1"]]


def kernel(**inputs):
    per_core = host_prep(inputs)
    for launch in PLAN:
        nc, es, in_names = build_program(launch)
        in_maps = [{k: per_core[c][k] for k in in_names} for c in range(N_CORES)]
        res = run_bass_kernel_spmd(nc, in_maps, core_ids=list(range(N_CORES)))
        es.close()
        for c in range(N_CORES):
            per_core[c]["x_in"] = np.asarray(res.results[c]["x_out"])
    out = np.concatenate([per_core[c]["x_in"] for c in range(N_CORES)], axis=0)
    return out.reshape(2, 8192, D).astype(np.float32)
